# version 23
# baseline (speedup 1.0000x reference)
import numpy as np
import ml_dtypes
import concourse.bass as bass
import concourse.mybir as mybir
from concourse.bass_utils import run_bass_kernel_spmd

F32 = mybir.dt.float32
BF16 = mybir.dt.bfloat16
AF = mybir.ActivationFunctionType
ALU = mybir.AluOpType
X = mybir.AxisListType.X

T_LEN = 2048
D = 1024
NH = 16
DH = 64
FF = 4096
NCORES = 8


class Ctx:

    def __init__(self, nc, sems):
        self.nc = nc
        self.sems = sems
        self.active = None
        self.eng = None
        self.cnt = None
        self.last = None

    def run(self, name, eng, emit_fn):
        self.active, self.eng = name, eng
        self.cnt = {k: 0 for k in self.sems}
        self.last = None
        emit_fn(self)

    def op(self, ename, method, *a, **k):
        if self.active == ename:
            self.last = getattr(self.eng, method)(*a, **k)
        else:
            self.last = None
        return self

    def inc(self, sem, n=1):
        self.cnt[sem] += n
        if self.last is not None:
            self.last.then_inc(self.sems[sem], n)
        self.last = None
        return self.cnt[sem]

    def wait(self, ename, sem, val=None):
        v = self.cnt[sem] if val is None else val
        if v <= 0:
            return
        if self.active == ename:
            self.eng.wait_ge(self.sems[sem], v)

    def now(self, sem):
        return self.cnt[sem]


def build_program(debug=False):
    nc = bass.Bass("TRN2", target_bir_lowering=False)

    dt_in = {}
    def din(name, shape, dtype=F32):
        dt_in[name] = nc.dram_tensor(name, shape, dtype, kind="ExternalInput")
        return dt_in[name]

    xT = din("xT", [D, T_LEN])
    wqkvT = din("wqkvT", [D, 3 * D])
    woutT = din("woutT", [D, D])
    gAd = din("gA", [D, 40])
    browd = din("brow", [1, 40])
    bcatTd = din("bcatT", [32, FF])
    wiTd = din("wiT", [D, FF])
    wib2d = din("wib2", [128, 32])
    woTd = din("woT", [FF, D])
    wob2d = din("wob2", [128, 8])
    idbd = din("idb", [128, 128], BF16)
    rpd = din("rp", [128, 128], BF16)
    rcosd = din("rcos", [128, T_LEN], BF16)
    rsind = din("rsin", [128, T_LEN], BF16)
    cmaskd = din("cmask", [128, 896], BF16)

    outT = nc.dram_tensor("outT", [D, T_LEN], F32, kind="ExternalOutput")
    dbg = {}
    if debug:
        dbg["QB0"] = nc.dram_tensor("dQB0", [128, 16384], BF16, kind="ExternalOutput")
        dbg["KB0"] = nc.dram_tensor("dKB0", [128, 16384], BF16, kind="ExternalOutput")
        dbg["VB0"] = nc.dram_tensor("dVB0", [128, 16640], BF16, kind="ExternalOutput")
        dbg["OSB"] = nc.dram_tensor("dOSB", [128, 4096], BF16, kind="ExternalOutput")
        dbg["Z1"] = nc.dram_tensor("dZ1", [128, 4096], BF16, kind="ExternalOutput")
        dbg["Y1"] = nc.dram_tensor("dY1", [128, 16384], BF16, kind="ExternalOutput")
        dbg["CT"] = nc.dram_tensor("dCT", [32, 2048], BF16, kind="ExternalOutput")
        dbg["INT"] = nc.dram_tensor("dINT", [128, 16384], BF16, kind="ExternalOutput")
        dbg["Z2"] = nc.dram_tensor("dZ2", [128, 4096], BF16, kind="ExternalOutput")
    wiS = nc.dram_tensor("wiS", [128, 8 * FF], BF16, kind="Internal")
    woS = nc.dram_tensor("woS", [128, 32 * D], BF16, kind="Internal")

    A = nc.alloc_sbuf_tensor
    XB = A("XB", [128, 16384], BF16)
    QB = A("QB", [128, 16384], BF16)
    KB = A("KB", [128, 16384], BF16)
    VB = A("VB", [128, 16640], BF16)
    AR = A("AR", [128, 18432], BF16)
    rcos = A("rcos_s", [128, T_LEN], BF16).ap()
    rsin = A("rsin_s", [128, T_LEN], BF16).ap()
    cmask = A("cmask_s", [128, 896], BF16).ap()
    idb = A("idb_s", [128, 128], BF16).ap()
    rp = A("rp_s", [128, 128], BF16).ap()
    z1 = A("z1", [128, 4096], BF16).ap()
    sq = A("sq", [128, 4096], BF16)
    rows = A("rows", [1, 2048], F32).ap()
    gA = A("gAs", [128, 320], BF16).ap()
    brow = A("brows", [1, 40], BF16).ap()
    wib2 = A("wib2s", [128, 32], F32).ap()
    wob2 = A("wob2s", [128, 8], F32).ap()
    onescol = A("onescol", [128, 1], BF16).ap()
    eps = A("eps", [1, 1], F32).ap()
    onesrow_b = A("onesrow_b", [1, 128], BF16).ap()
    onesrow_f = A("onesrow_f", [1, 128], F32).ap()
    aux = A("aux", [128, 16], F32).ap()
    den = A("den", [128, 8], F32).ap()
    mask_f = A("mask_f", [128, 8], F32).ap()
    mask_b = A("mask_b", [128, 8], BF16).ap()
    c_st = A("c_st", [128, 32], BF16).ap()
    cT = A("cT", [32, 2048], BF16).ap()
    p3 = A("p3", [128, 1024], BF16).ap()

    XBa, QBa, KBa, VBa, ARa, sqa = XB.ap(), QB.ap(), KB.ap(), VB.ap(), AR.ap(), sq.ap()
    VBf = VB.bitcast(F32)

    wq_st = ARa[:, 0:12288].rearrange("p (dc o) -> p dc o", o=1536)
    wout_sb = ARa[:, 0:8192]
    o_sb = ARa[:, 8192:12288]
    oT_sb = ARa[:, 12288:16384]
    p_sb = [ARa[:, 16384:17408], ARa[:, 17408:18432], p3]
    wi_buf = [KBa[:, 0:4096], KBa[:, 4096:8192]]
    bcat = KBa[0:32, 8192:12288]

    z2 = VBa[:, 0:4096]
    y2 = VBf[:, 2048:6144]
    wo_buf = [ARa[:, 0:8192], ARa[:, 8192:16384]]

    praw = [sqa[:, 0:512], sqa[:, 512:1024]]
    swp = [sqa[:, 1024:1536], sqa[:, 1536:2048]]
    t1b = [sqa[:, 2048:2560], sqa[:, 2560:3072]]
    t2b = [sqa[:, 3072:3584], sqa[:, 3584:4096]]

    r_mean, r_var, r_tmp, r_nmr = (rows[:, 0:512], rows[:, 512:1024],
                                   rows[:, 1024:1536], rows[:, 1536:2048])

    P = nc.alloc_psum_tensor
    psA = P("psA", [128, 1024], F32).ap()
    psB = P("psB", [128, 1024], F32).ap()
    psC = P("psC", [128, 1024], F32).ap()
    psD = P("psD", [128, 512], F32).ap()
    psE = P("psE", [128, 1024], BF16).ap()
    rot4 = [psA[:, 0:512], psA[:, 512:1024], psB[:, 0:512], psB[:, 512:1024]]
    psC_h = [psC[:, 0:512], psC[:, 512:1024]]

    sems = {n: nc.alloc_semaphore(n) for n in
            ["ld", "lg", "kql", "vql", "wol", "kq2", "vq2", "wo2", "wis", "wos",
             "bct", "li0", "li1", "lo0", "lo1", "pe", "ac", "ve", "po", "st"]}

    def emit(c: Ctx):
        Tn, Sc, Ve, Gp, Sy = "tensor", "scalar", "vector", "gpsimd", "sync"

        c.op(Sy, "dma_start", idb, idbd.ap()); c.inc("ld", 16)
        c.op(Sy, "dma_start", rp, rpd.ap()); c.inc("ld", 16)
        c.op(Sy, "dma_start", rcos, rcosd.ap()); c.inc("ld", 16)
        c.op(Sy, "dma_start", rsin, rsind.ap()); c.inc("ld", 16)
        c.op(Sy, "dma_start", cmask, cmaskd.ap()); c.inc("ld", 16)
        c.op(Sy, "dma_start", wib2, wib2d.ap()); c.inc("ld", 16)
        c.op(Sy, "dma_start", wob2, wob2d.ap()); c.inc("ld", 16)

        c.op(Gp, "dma_start", XBa.rearrange("p (dc t) -> p dc t", t=T_LEN),
             xT.rearrange("(dc p) t -> p dc t", p=128)); c.inc("lg", 16)
        wq_r = wqkvT.rearrange("(dc p) o -> p dc o", p=128)
        c.op(Gp, "dma_start", wq_st[:, :, 0:512], wq_r[:, :, 0:512])
        WQ_P1 = ("kql", c.inc("kql", 16))
        c.op(Gp, "dma_start", wq_st[:, :, 512:1024], wq_r[:, :, 512:1024])
        WQ_P2 = ("vql", c.inc("vql", 16))
        c.op(Gp, "dma_start", wq_st[:, :, 1024:1536], wq_r[:, :, 1024:1536])
        WQ_P3 = ("wol", c.inc("wol", 16))
        c.op(Gp, "dma_start", gA.rearrange("p (dc e) -> p dc e", e=40),
             gAd.rearrange("(dc p) e -> p dc e", p=128)); c.inc("lg", 16)
        c.op(Gp, "dma_start", brow, browd.ap()); c.inc("lg", 16)
        LD_ALL = c.now("ld")
        LG_ALL = c.now("lg")

        c.op(Ve, "memset",
             VBa.rearrange("p (kt h x) -> p kt h x", h=16, x=65)[:, :, :, 64:65], 1.0)
        c.inc("ve")
        c.op(Ve, "memset", onescol, 1.0); c.inc("ve")
        c.op(Ve, "memset", onesrow_b, 1.0); c.inc("ve")
        c.op(Ve, "memset", onesrow_f, 1.0); c.inc("ve")
        c.op(Ve, "memset", eps, 1e-5); c.inc("ve")
        MEMSET_DONE = c.now("ve")

        slot_free_ac = [("ac", 0), ("ac", 0), ("ac", 0), ("ac", 0)]
        pc_free = [0, 0]
        praw_guard = [0, 0]
        po_guard = [0, 0]
        idx = 0
        QKV_A_MARK = None
        pend = []
        t1_guard = [0, 0]
        add_guard = [0, 0]

        def emit_rope(item):
            idx_, oc_, tg_, ev_praw_ = item
            pr_ = idx_ % 2
            h2 = idx_ % 2
            c.wait(Tn, "ac", ev_praw_)
            c.wait(Tn, "ac", pc_free[h2])
            c.op(Tn, "matmul", psC_h[h2], rp, praw[pr_], start=True, stop=True)
            pe_r = c.inc("pe")
            praw_guard[pr_] = pe_r
            c.wait(Sc, "pe", pe_r)
            c.wait(Sc, "po", po_guard[h2])
            c.op(Sc, "copy", swp[h2], psC_h[h2])
            ev_swp = c.inc("ac")
            pc_free[h2] = ev_swp
            c.wait(Ve, "ac", ev_praw_)
            c.op(Ve, "tensor_mul", t1b[h2], praw[pr_],
                 rcos[:, tg_ * 512:(tg_ + 1) * 512])
            ev_t1 = c.inc("ve")
            c.wait(Gp, "ac", ev_swp)
            c.wait(Gp, "ve", add_guard[h2])
            c.op(Gp, "tensor_mul", t2b[h2], swp[h2],
                 rsin[:, tg_ * 512:(tg_ + 1) * 512])
            po_t2 = c.inc("po")
            po_guard[h2] = po_t2
            dst = (QBa[:, oc_ * 2048 + tg_ * 512: oc_ * 2048 + (tg_ + 1) * 512]
                   if oc_ < 8 else
                   KBa[:, (oc_ - 8) * 2048 + tg_ * 512: (oc_ - 8) * 2048 + (tg_ + 1) * 512])
            c.wait(Ve, "ve", ev_t1)
            c.wait(Ve, "po", po_t2)
            c.op(Ve, "tensor_add", dst, t1b[h2], t2b[h2])
            add_guard[h2] = c.inc("ve")
            t1_guard[h2] = ev_t1

        LW_K = LW_V = None
        for oc in range(16):
            half2 = oc >= 12
            for tg in range(4):
                r = idx % 4
                c.wait(Tn, "ld", LD_ALL)
                c.wait(Tn, "lg", LG_ALL)
                if oc < 4:
                    c.wait(Tn, *WQ_P1)
                elif oc < 8:
                    c.wait(Tn, *WQ_P2)
                elif oc < 12:
                    c.wait(Tn, *WQ_P3)
                else:
                    c.wait(Tn, *LW_K)
                c.wait(Tn, slot_free_ac[r][0], slot_free_ac[r][1])
                for dc in range(8):
                    if oc < 8:
                        ocol = oc * 128
                    elif oc < 12:
                        ocol = 1024 + (oc - 8) * 128
                    else:
                        ocol = (oc - 12) * 128
                    c.op(Tn, "matmul", rot4[r],
                         wq_st[:, dc, ocol:ocol + 128],
                         XBa[:, dc * 2048 + tg * 512: dc * 2048 + (tg + 1) * 512],
                         start=(dc == 0), stop=(dc == 7))
                pe_proj = c.inc("pe")
                pr = idx % 2
                c.wait(Sc, "pe", pe_proj)
                c.wait(Sc, "pe", praw_guard[pr])
                c.wait(Sc, "ve", t1_guard[pr])
                c.op(Sc, "copy", praw[pr], rot4[r])
                ev_praw = c.inc("ac")
                slot_free_ac[r] = ("ac", ev_praw)
                pend.append((idx, oc, tg, ev_praw))
                if len(pend) >= 2:
                    emit_rope(pend.pop(0))
                idx += 1
            if oc == 3:
                c.wait(Gp, "pe", c.now("pe"))
                c.op(Gp, "dma_start", wq_st[:, :, 0:512],
                     wqkvT.rearrange("(dc p) o -> p dc o", p=128)[:, :, 1536:2048])
                LW_K = ("kq2", c.inc("kq2", 16))
            if oc == 11:
                for item in pend:
                    emit_rope(item)
                pend = []
                c.wait(Gp, "pe", c.now("pe"))
                c.op(Gp, "dma_start", wq_st[:, :, 512:1536],
                     wqkvT.rearrange("(dc p) o -> p dc o", p=128)[:, :, 2048:3072])
                LW_V = ("vq2", c.inc("vq2", 16))

        for item in pend:
            emit_rope(item)
        pend = []

        vidx = 0
        for tt in range(16):
            for sl in range(2):
                r = vidx % 4
                c.wait(Tn, slot_free_ac[r][0], slot_free_ac[r][1])
                c.wait(Tn, *LW_V)
                for dc in range(8):
                    c.op(Tn, "matmul", rot4[r],
                         XBa[:, dc * 2048 + tt * 128: dc * 2048 + (tt + 1) * 128],
                         wq_st[:, dc, 512 + sl * 512: 512 + (sl + 1) * 512],
                         start=(dc == 0), stop=(dc == 7))
                pe_v = c.inc("pe")
                c.wait(Ve, "pe", pe_v)
                c.wait(Ve, "ve", MEMSET_DONE)
                base = tt * 1040 + sl * 8 * 65
                c.op(Ve, "tensor_copy",
                     VBa[:, base:base + 520].rearrange("p (h x) -> p h x", x=65)[:, :, 0:64],
                     rot4[r].rearrange("p (h x) -> p h x", x=64))
                slot_free_ac[r] = ("ve", c.inc("ve"))
                vidx += 1
        V_DONE_PE = c.now("pe")
        ROPE_V_DONE = c.now("ve")
        ROPE_V_AC = c.now("ac")
        if debug:
            c.wait(Sy, "ve", ROPE_V_DONE)
            c.op(Sy, "dma_start", dbg["QB0"].ap(), QBa); c.inc("st", 16)
            c.op(Sy, "dma_start", dbg["KB0"].ap(), KBa); c.inc("st", 16)
            c.op(Sy, "dma_start", dbg["VB0"].ap(), VBa); c.inc("st", 16)

        c.wait(Gp, "pe", V_DONE_PE)
        c.op(Gp, "dma_start", wout_sb.rearrange("p (dc o) -> p dc o", o=1024),
             woutT.rearrange("(dc p) o -> p dc o", p=128))
        WOUT_LW = c.inc("wo2", 16)
        c.op(Gp, "dma_start", wiS.rearrange("p (dc f) -> p dc f", f=FF),
             wiTd.rearrange("(dc p) f -> p dc f", p=128))
        WIS_LW = c.inc("wis", 16)
        c.op(Gp, "dma_start", woS.rearrange("p (fc d) -> p fc d", d=D),
             woTd.rearrange("(fc p) d -> p fc d", p=128))
        WOS_LW = c.inc("wos", 16)

        av_prev_free = 0
        exp_free = [0, 0]
        state = {"psD_ve": 0, "psE_ve": 0, "sq_ve": 0}
        ve_y = [0, 0, 0, 0]
        ar_mark = [0, 0, 0, 0]
        z1_mark = [0, 0, 0, 0]
        ct_mark = [0, 0, 0, 0]
        tp_mark = [0, 0, 0, 0]
        av_prev_pe = 0

        def tail_gen(g, O_D):
            for grp in range(4):
                c.wait(Tn, "ve", state["psE_ve"])
                if grp == 0:
                    c.wait(Tn, "ve", O_D)
                for dc in range(8):
                    c.op(Tn, "transpose", psE[:, dc * 128:(dc + 1) * 128],
                         o_sb[:, grp * 1024 + dc * 128: grp * 1024 + (dc + 1) * 128],
                         idb)
                    if dc == 3:
                        yield
                pe_t = c.inc("pe")
                c.wait(Ve, "pe", pe_t)
                for dc in range(8):
                    c.op(Ve, "tensor_copy",
                         oT_sb[:, dc * 512 + grp * 128: dc * 512 + (grp + 1) * 128],
                         psE[:, dc * 128:(dc + 1) * 128])
                state["psE_ve"] = c.inc("ve")
                yield
            tp_mark[g] = state["psE_ve"]
            OT_DONE = state["psE_ve"]
            sq_guard = state["sq_ve"]
            for doc in range(8):
                c.wait(Tn, "wo2", WOUT_LW)
                c.wait(Tn, "ve", state["psD_ve"])
                if doc == 0:
                    c.wait(Tn, "ve", OT_DONE)
                for dmc in range(8):
                    c.op(Tn, "matmul", psD,
                         wout_sb[:, dmc * 1024 + doc * 128: dmc * 1024 + (doc + 1) * 128],
                         oT_sb[:, dmc * 512:(dmc + 1) * 512],
                         start=(dmc == 0), stop=(dmc == 7))
                    if dmc == 3:
                        yield
                pe_op = c.inc("pe")
                c.wait(Ve, "pe", pe_op)
                c.wait(Ve, "ve", sq_guard)
                if debug and g == 1 and doc == 0:
                    c.wait(Ve, "st", c.now("st"))
                c.op(Ve, "tensor_add",
                     z1[:, doc * 512:(doc + 1) * 512], psD,
                     XBa[:, doc * 2048 + g * 512: doc * 2048 + (g + 1) * 512])
                state["psD_ve"] = c.inc("ve")
                yield
            VE_Z = c.now("ve")
            z1_mark[g] = VE_Z
            ar_mark[g] = c.now("pe")
            if debug and g == 0:
                c.wait(Sy, "ve", VE_Z)
                c.op(Sy, "dma_start", dbg["Z1"].ap(), z1); c.inc("st", 16)
            c.wait(Ve, "ve", VE_Z)
            c.op(Ve, "tensor_mul", sqa[:], z1, z1)
            ac_q = c.inc("ve")
            state["sq_ve"] = ac_q
            c.wait(Tn, "ve", VE_Z)
            c.wait(Tn, "ve", state["psD_ve"])
            for dc in range(8):
                c.op(Tn, "matmul", psD[0:1, :], onescol,
                     z1[:, dc * 512:(dc + 1) * 512], start=(dc == 0), stop=(dc == 7))
                if dc % 2 == 1 and dc < 7:
                    yield
            pe_s1 = c.inc("pe")
            c.wait(Ve, "pe", pe_s1)
            c.op(Ve, "tensor_scalar", r_mean, psD[0:1, :], 1.0 / D, None, ALU.mult)
            state["psD_ve"] = c.inc("ve")
            yield
            yield
            c.wait(Tn, "ve", ac_q)
            c.wait(Tn, "ve", state["psD_ve"])
            for dc in range(8):
                c.op(Tn, "matmul", psD[0:1, :], onescol,
                     sqa[:, dc * 512:(dc + 1) * 512], start=(dc == 0), stop=(dc == 7))
                if dc % 2 == 1 and dc < 7:
                    yield
            pe_s2 = c.inc("pe")
            c.wait(Ve, "pe", pe_s2)
            c.op(Ve, "tensor_scalar", r_var, psD[0:1, :], 1.0 / D, None, ALU.mult)
            v1 = c.inc("ve")
            state["psD_ve"] = v1
            c.wait(Ve, "ve", v1)
            c.op(Ve, "tensor_mul", r_tmp, r_mean, r_mean)
            v2 = c.inc("ve")
            c.wait(Ve, "ve", v2)
            c.op(Ve, "tensor_sub", r_var, r_var, r_tmp)
            ve_var = c.inc("ve")
            c.wait(Sc, "ve", ve_var)
            c.op(Sc, "activation", r_tmp, r_var, AF.Ln, bias=eps)
            a1 = c.inc("ac")
            c.wait(Sc, "ac", a1)
            c.op(Sc, "activation", r_tmp, r_tmp, AF.Exp, bias=0.0, scale=-0.5)
            ac_rs = c.inc("ac")
            c.wait(Ve, "ac", ac_rs)
            c.op(Ve, "scalar_tensor_tensor", r_nmr, r_mean, -1.0, r_tmp,
                 ALU.mult, ALU.mult)
            ve_nm = c.inc("ve")
            yield
            yield
            yield
            c.wait(Tn, "ve", ve_nm)
            c.wait(Tn, "ve", state["psD_ve"])
            c.op(Tn, "matmul", psD, onesrow_f, r_tmp, start=True, stop=True)
            pe_b1 = c.inc("pe")
            c.wait(Ve, "pe", pe_b1)
            for dc in range(8):
                c.op(Ve, "tensor_mul", sqa[:, dc * 512:(dc + 1) * 512],
                     z1[:, dc * 512:(dc + 1) * 512], psD)
            state["psD_ve"] = c.inc("ve")
            ve_t1m = state["psD_ve"]
            yield
            c.wait(Tn, "ve", state["psD_ve"])
            c.op(Tn, "matmul", psD, onesrow_f, r_nmr, start=True, stop=True)
            pe_b2 = c.inc("pe")
            c.wait(Ve, "pe", pe_b2)
            c.wait(Ve, "ve", ve_t1m)
            for dc in range(8):
                c.op(Ve, "tensor_add",
                     QBa[:, dc * 2048 + g * 512: dc * 2048 + (g + 1) * 512],
                     sqa[:, dc * 512:(dc + 1) * 512], psD)
            ve_y[g] = c.inc("ve")
            state["psD_ve"] = ve_y[g]
            yield
            pe_ct_prev = 0
            for tt in range(4):
                c.wait(Tn, "ve", ve_y[g])
                c.wait(Tn, "ve", state["psD_ve"])
                for dc in range(8):
                    c.op(Tn, "matmul", psD[:, 0:40],
                         QBa[:, dc * 2048 + g * 512 + tt * 128:
                             dc * 2048 + g * 512 + (tt + 1) * 128],
                         gA[:, dc * 40:(dc + 1) * 40],
                         start=(dc == 0), stop=False, skip_group_check=True)
                c.op(Tn, "matmul", psD[:, 0:40], onesrow_b, brow,
                     start=False, stop=True, skip_group_check=True)
                pe_g = c.inc("pe")
                c.wait(Ve, "pe", pe_g)
                c.op(Ve, "tensor_copy", aux[:, 0:8], psD[:, 0:8])
                s1 = c.inc("ve"); c.wait(Ve, "ve", s1)
                c.op(Ve, "reduce_max", aux[:, 8:9], aux[:, 0:8], axis=X)
                s2 = c.inc("ve"); c.wait(Ve, "ve", s2)
                c.op(Ve, "tensor_scalar", mask_f, aux[:, 0:8], aux[:, 8:9], 0.0,
                     ALU.is_ge, ALU.add, accum_out=aux[:, 9:10])
                s3 = c.inc("ve"); c.wait(Ve, "ve", s3)
                c.op(Ve, "reciprocal", aux[:, 10:11], aux[:, 9:10])
                s4 = c.inc("ve"); c.wait(Ve, "ve", s4)
                c.op(Ve, "tensor_scalar", mask_b, mask_f, aux[:, 10:11], None, ALU.mult)
                s5 = c.inc("ve"); c.wait(Ve, "ve", s5)
                c.wait(Ve, "pe", pe_ct_prev)
                c.op(Ve, "tensor_tensor",
                     c_st.rearrange("p (e r) -> p e r", e=8),
                     psD[:, 8:40].rearrange("p (e r) -> p e r", e=8),
                     mask_b[:, :, None].broadcast_to([128, 8, 4]),
                     ALU.mult)
                ve_c = c.inc("ve")
                state["psD_ve"] = ve_c
                yield
                c.wait(Tn, "ve", ve_c)
                c.wait(Tn, "ve", state["psE_ve"])
                c.op(Tn, "transpose", psE[0:32, 0:128], c_st, idb)
                pe_ct = c.inc("pe")
                pe_ct_prev = pe_ct
                c.wait(Ve, "pe", pe_ct)
                c.op(Ve, "tensor_copy",
                     cT[:, g * 512 + tt * 128: g * 512 + (tt + 1) * 128],
                     psE[0:32, 0:128])
                state["psE_ve"] = c.inc("ve")
                ct_mark[g] = state["psE_ve"]
                yield

        tail = None
        pump = [0]
        for g in range(4):
            ktmax = 4 * g + 4
            if g == 0:
                c.wait(Tn, "ve", ROPE_V_DONE)
                c.wait(Tn, "ac", ROPE_V_AC)
            for cc in range(8):
                pe_s = {}
                ac_e = {}
                ve_m = {}
                av_done = {}

                def emit_av(kt):
                    pp = kt % 3
                    c.wait(Tn, "ac", ac_e[kt])
                    j = kt - 4 * g
                    if kt == 0:
                        c.wait(Tn, "ve", av_prev_free)
                    first_unmasked = 1 if (kt == 0 and j == 0) else 0
                    for masked in (False, True):
                        if masked:
                            if j < 0:
                                continue
                            c.wait(Tn, "ve", ve_m[kt])
                        for hl in (0, 1):
                            for qt in range(4):
                                if j >= 0 and qt < j:
                                    continue
                                if (qt == j) != masked:
                                    continue
                                col = hl * 512 + qt * 65
                                c.op(Tn, "matmul", psC[:, col: col + 65],
                                     p_sb[pp][:, hl * 512 + qt * 128:
                                               hl * 512 + (qt + 1) * 128],
                                     VBa[:, kt * 1040 + (2 * cc + hl) * 65:
                                         kt * 1040 + (2 * cc + hl) * 65 + 65],
                                     start=(kt == 0 and not masked
                                            and qt == first_unmasked),
                                     stop=(kt == ktmax - 1),
                                     skip_group_check=True)
                    av_done[kt] = c.inc("pe")

                for kt in range(ktmax):
                    pp2 = kt % 2
                    pp = kt % 3
                    psS = psA if pp2 == 0 else psB
                    c.wait(Tn, "ac", exp_free[pp2])
                    pump[0] += 1
                    for _ in range(2 if tail is None else 1):
                        c.op(Tn, "matmul", psS[:, 0:512], idb,
                             QBa[:, 0:512], start=True, stop=True)
                    for hl in (0, 1):
                        c.op(Tn, "matmul", psS[:, hl * 512:(hl + 1) * 512],
                             KBa[hl * 64:(hl + 1) * 64,
                                 cc * 2048 + kt * 128: cc * 2048 + (kt + 1) * 128],
                             QBa[hl * 64:(hl + 1) * 64,
                                 cc * 2048 + g * 512: cc * 2048 + (g + 1) * 512],
                             start=True, stop=True)
                    pe_s[kt] = c.inc("pe")
                    c.wait(Sc, "pe", pe_s[kt])
                    c.wait(Sc, "pe", av_done[kt - 3] if kt >= 3 else av_prev_pe)
                    jj = kt - 4 * g
                    if jj > 0:
                        c.op(Sc, "activation",
                             p_sb[pp].rearrange("p (h q) -> p h q", q=512)
                             [:, :, jj * 128:512],
                             psS.rearrange("p (h q) -> p h q", q=512)
                             [:, :, jj * 128:512],
                             AF.Exp, bias=0.0, scale=0.125)
                    else:
                        c.op(Sc, "activation", p_sb[pp], psS, AF.Exp,
                             bias=0.0, scale=0.125)
                    ac_e[kt] = c.inc("ac")
                    exp_free[pp2] = ac_e[kt]
                    if kt >= 4 * g:
                        j = kt - 4 * g
                        c.wait(Ve, "ac", ac_e[kt])
                        c.op(Ve, "tensor_tensor",
                             p_sb[pp].rearrange("p (h q) -> p h q", q=512)
                             [:, :, j * 128:(j + 1) * 128],
                             p_sb[pp].rearrange("p (h q) -> p h q", q=512)
                             [:, :, j * 128:(j + 1) * 128],
                             cmask[:, 384:512][:, None, :].broadcast_to([128, 2, 128]),
                             ALU.mult)
                        ve_m[kt] = c.inc("ve")
                    if tail is not None:
                        next(tail, None)
                    if kt >= 2:
                        emit_av(kt - 2)
                emit_av(ktmax - 2)
                emit_av(ktmax - 1)
                av_prev_pe = av_done[ktmax - 1]
                if g >= 1 and cc == 0:
                    if tail is not None:
                        while tp_mark[g - 1] == 0:
                            if next(tail, "END") == "END":
                                break
                c.wait(Ve, "pe", av_done[ktmax - 1])
                if cc == 0 and g >= 1:
                    c.wait(Ve, "ve", tp_mark[g - 1])
                    if debug:
                        c.wait(Ve, "st", c.now("st"))
                psC_v = (psC.rearrange("p (hl x) -> p hl x", hl=2)[:, :, 0:260]
                         .rearrange("p hl (qt d) -> p hl qt d", d=65))
                c.op(Ve, "reciprocal",
                     den.rearrange("p (hl qt) -> p hl qt", hl=2)[:, :, :, None],
                     psC_v[:, :, :, 64:65])
                ve_d = c.inc("ve")
                c.wait(Ve, "ve", ve_d)
                o_dst = (o_sb.rearrange("p (qt h2 d) -> p qt h2 d", h2=16, d=64)
                         [:, :, 2 * cc:2 * cc + 2, :].transpose([0, 2, 1, 3]))
                c.op(Ve, "tensor_tensor", o_dst,
                     psC_v[:, :, :, 0:64],
                     den.rearrange("p (hl qt) -> p hl qt", hl=2)[:, :, :, None]
                     .broadcast_to([128, 2, 4, 64]),
                     ALU.mult)
                av_prev_free = c.inc("ve")
            if debug and g == 0:
                c.wait(Sy, "ve", av_prev_free)
                c.op(Sy, "dma_start", dbg["OSB"].ap(), o_sb); c.inc("st", 16)
            if tail is not None:
                for _ in tail:
                    pass
            tail = tail_gen(g, c.now("ve"))

        c.wait(Gp, "pe", c.now("pe"))
        c.op(Gp, "dma_start", bcat, bcatTd.ap())
        BCAT_LW = c.inc("bct", 16)

        wi_buf_free = [c.now("pe"), c.now("pe")]
        wo_buf_free = [c.now("pe"), c.now("pe")]
        lw_wi = {}
        lw_wo = {}
        nxt_lw0 = []
        relu_free = [exp_free[0], exp_free[0], exp_free[1], exp_free[1]]
        z2ev = [av_prev_free, av_prev_free]
        ve_y2 = 0
        ST_PREV = 0
        wiS_v = wiS.rearrange("p (dc f) -> p dc f", f=FF)
        woS_v = woS.rearrange("p (fc d) -> p fc d", d=D)
        st_box = [0]

        def ln2_gen(g, VE_Z2):
            c.wait(Ve, "ve", VE_Z2)
            c.op(Ve, "tensor_mul", sqa[:], z2, z2)
            ve_q2 = c.inc("ve")
            c.wait(Tn, "ve", VE_Z2)
            c.wait(Tn, "ve", state["psD_ve"])
            for dc in range(8):
                c.op(Tn, "matmul", psD[0:1, :], onescol,
                     z2[:, dc * 512:(dc + 1) * 512], start=(dc == 0), stop=(dc == 7))
            pe_s1 = c.inc("pe")
            c.wait(Ve, "pe", pe_s1)
            c.op(Ve, "tensor_scalar", r_mean, psD[0:1, :], 1.0 / D, None, ALU.mult)
            state["psD_ve"] = c.inc("ve")
            yield
            c.wait(Tn, "ve", ve_q2)
            c.wait(Tn, "ve", state["psD_ve"])
            for dc in range(8):
                c.op(Tn, "matmul", psD[0:1, :], onescol,
                     sqa[:, dc * 512:(dc + 1) * 512], start=(dc == 0), stop=(dc == 7))
            pe_s2 = c.inc("pe")
            c.wait(Ve, "pe", pe_s2)
            c.op(Ve, "tensor_scalar", r_var, psD[0:1, :], 1.0 / D, None, ALU.mult)
            v1 = c.inc("ve")
            state["psD_ve"] = v1
            c.wait(Ve, "ve", v1)
            c.op(Ve, "tensor_mul", r_tmp, r_mean, r_mean)
            v2 = c.inc("ve")
            c.wait(Ve, "ve", v2)
            c.op(Ve, "tensor_sub", r_var, r_var, r_tmp)
            ve_var = c.inc("ve")
            c.wait(Sc, "ve", ve_var)
            c.op(Sc, "activation", r_tmp, r_var, AF.Ln, bias=eps)
            a1 = c.inc("ac")
            c.wait(Sc, "ac", a1)
            c.op(Sc, "activation", r_tmp, r_tmp, AF.Exp, bias=0.0, scale=-0.5)
            ac_rs = c.inc("ac")
            c.wait(Ve, "ac", ac_rs)
            c.op(Ve, "scalar_tensor_tensor", r_nmr, r_mean, -1.0, r_tmp,
                 ALU.mult, ALU.mult)
            ve_nm = c.inc("ve")
            yield
            yield
            c.wait(Tn, "ve", ve_nm)
            c.wait(Tn, "ve", state["psD_ve"])
            c.op(Tn, "matmul", psD, onesrow_f, r_tmp, start=True, stop=True)
            pe_b1 = c.inc("pe")
            c.wait(Ve, "pe", pe_b1)
            c.wait(Ve, "st", st_box[0])
            for dc in range(8):
                c.op(Ve, "tensor_mul", sqa[:, dc * 512:(dc + 1) * 512],
                     z2[:, dc * 512:(dc + 1) * 512], psD)
            ve_t2 = c.inc("ve")
            state["psD_ve"] = ve_t2
            yield
            c.wait(Tn, "ve", state["psD_ve"])
            c.op(Tn, "matmul", psD, onesrow_f, r_nmr, start=True, stop=True)
            pe_b2 = c.inc("pe")
            c.wait(Ve, "pe", pe_b2)
            c.wait(Ve, "ve", ve_t2)
            for dc in range(8):
                c.op(Ve, "tensor_add", y2[:, dc * 512:(dc + 1) * 512],
                     sqa[:, dc * 512:(dc + 1) * 512], psD)
            ve_y2 = c.inc("ve")
            state["psD_ve"] = ve_y2
            c.wait(Sy, "ve", ve_y2)
            c.op(Sy, "dma_start",
                 outT.rearrange("(dc p) t -> p dc t", p=128)[:, :, g * 512:(g + 1) * 512],
                 y2.rearrange("p (dc t) -> p dc t", t=512))
            st_box[0] = c.inc("st", 16)
            yield

        ln2 = None
        for g in range(4):
            if g == 0:
                for e8 in (0, 1):
                    c.wait(Sy, "pe", wi_buf_free[e8 % 2])
                    c.wait(Sy, "wis", WIS_LW)
                    c.op(Sy, "dma_start",
                         wi_buf[e8 % 2].rearrange("p (dc f) -> p dc f", f=512),
                         wiS_v[:, :, e8 * 512:(e8 + 1) * 512])
                    lw_wi[e8] = ("li%d" % (e8 % 2), c.inc("li%d" % (e8 % 2), 16))
            if debug and g == 1:
                c.wait(Sc, "st", c.now("st"))
            if tail is not None:
                while z1_mark[3] == 0:
                    if next(tail, "END") == "END":
                        break
            for e8 in range(8):
                c.wait(Tn, *lw_wi[e8])
                c.wait(Tn, "bct", BCAT_LW)
                for fcl in range(4):
                    fc = e8 * 4 + fcl
                    r = fc % 4
                    c.wait(Tn, "ac", relu_free[r])
                    c.wait(Tn, "ve", ve_y[g])
                    c.wait(Tn, "ve", ct_mark[g])
                    for dc in range(8):
                        c.op(Tn, "matmul", rot4[r],
                             wi_buf[e8 % 2][:, dc * 512 + fcl * 128:
                                            dc * 512 + (fcl + 1) * 128],
                             QBa[:, dc * 2048 + g * 512: dc * 2048 + (g + 1) * 512],
                             start=(dc == 0), stop=False)
                    c.op(Tn, "matmul", rot4[r],
                         bcat[:, fc * 128:(fc + 1) * 128],
                         cT[:, g * 512:(g + 1) * 512],
                         start=False, stop=True)
                    pe_wi = c.inc("pe")
                    c.wait(Sc, "pe", pe_wi)
                    if g == 0 and fc == 0:
                        c.wait(Sc, "ve", z1_mark[3])
                    c.op(Sc, "activation", XBa[:, fc * 512:(fc + 1) * 512],
                         rot4[r], AF.Relu, bias=wib2[:, fc:fc + 1])
                    relu_free[r] = c.inc("ac")
                    if tail is not None:
                        next(tail, None)
                    if ln2 is not None:
                        next(ln2, None)
                wi_buf_free[e8 % 2] = c.now("pe")
                nxt = e8 + 2
                if nxt <= 7:
                    c.wait(Sy, "pe", wi_buf_free[nxt % 2])
                    c.op(Sy, "dma_start",
                         wi_buf[nxt % 2].rearrange("p (dc f) -> p dc f", f=512),
                         wiS_v[:, :, nxt * 512:(nxt + 1) * 512])
                    lw_wi[nxt] = ("li%d" % (nxt % 2), c.inc("li%d" % (nxt % 2), 16))
                elif g < 3:
                    ne = nxt - 8
                    c.wait(Sy, "pe", wi_buf_free[ne % 2])
                    c.op(Sy, "dma_start",
                         wi_buf[ne % 2].rearrange("p (dc f) -> p dc f", f=512),
                         wiS_v[:, :, ne * 512:(ne + 1) * 512])
                    lw_wi[ne] = ("li%d" % (ne % 2), c.inc("li%d" % (ne % 2), 16))
            if tail is not None:
                for _ in tail:
                    pass
                tail = None
                wo_buf_free[0] = c.now("pe")
                wo_buf_free[1] = c.now("pe")
            if ln2 is not None:
                for _ in ln2:
                    pass
                ln2 = None
                if debug:
                    c.wait(Sy, "ve", c.now("ve"))
                    c.op(Sy, "dma_start", dbg["Y1"].ap(), QBa); c.inc("st", 16)
                    c.op(Sy, "dma_start", dbg["CT"].ap(), cT); c.inc("st", 16)

            if debug and g == 0:
                c.wait(Sy, "ac", c.now("ac"))
                c.op(Sy, "dma_start", dbg["INT"].ap(), XBa); c.inc("st", 16)
            if g == 0:
                c.wait(Sy, "pe", wo_buf_free[0])
                c.wait(Sy, "pe", ar_mark[3])
                c.wait(Sy, "wos", WOS_LW)
                c.op(Sy, "dma_start",
                     wo_buf[0].rearrange("p (fc d) -> p fc d", d=256),
                     woS_v[:, :, 0:256])
                lw_wo[0] = ("lo0", c.inc("lo0", 16))
            for doc in range(8):
                dp = doc // 2
                if doc % 2 == 0 and dp + 1 <= 3:
                    c.wait(Sy, "pe", wo_buf_free[(dp + 1) % 2])
                    c.op(Sy, "dma_start",
                         wo_buf[(dp + 1) % 2].rearrange("p (fc d) -> p fc d", d=256),
                         woS_v[:, :, (dp + 1) * 256:(dp + 2) * 256])
                    lw_wo[dp + 1] = ("lo%d" % ((dp + 1) % 2), c.inc("lo%d" % ((dp + 1) % 2), 16))
                if doc == 6 and g < 3:
                    nxt_lw0.append(c.now("pe"))
                    c.wait(Sy, "pe", nxt_lw0[-1])
                    c.op(Sy, "dma_start",
                         wo_buf[0].rearrange("p (fc d) -> p fc d", d=256),
                         woS_v[:, :, 0:256])
                    lw_wo[0] = ("lo0", c.inc("lo0", 16))
                c.wait(Tn, *lw_wo[dp])
                h2 = doc % 2
                c.wait(Tn, "ve", z2ev[h2])
                c.wait(Tn, "ac", c.now("ac") if doc == 0 else 0)
                for fc in range(32):
                    c.op(Tn, "matmul", psC_h[h2],
                         wo_buf[dp % 2][:, fc * 256 + (doc % 2) * 128:
                                        fc * 256 + (doc % 2) * 128 + 128],
                         XBa[:, fc * 512:(fc + 1) * 512],
                         start=(fc == 0), stop=(fc == 31))
                pe_wo = c.inc("pe")
                if doc % 2 == 1:
                    wo_buf_free[dp % 2] = pe_wo
                c.wait(Ve, "pe", pe_wo)
                c.op(Ve, "scalar_tensor_tensor",
                     z2[:, doc * 512:(doc + 1) * 512],
                     psC_h[h2], wob2[:, doc:doc + 1],
                     QBa[:, doc * 2048 + g * 512: doc * 2048 + (g + 1) * 512],
                     ALU.add, ALU.add)
                z2ev[h2] = c.inc("ve")
            VE_Z2 = c.now("ve")
            if debug and g == 0:
                c.wait(Sy, "ve", VE_Z2)
                c.op(Sy, "dma_start", dbg["Z2"].ap(), z2); c.inc("st", 16)
            if debug and g == 1:
                c.wait(Ve, "st", c.now("st"))
            ln2 = ln2_gen(g, VE_Z2)
            if g == 3:
                for _ in ln2:
                    pass
                ln2 = None

        c.wait(Sy, "st", c.now("st"))

    ctx = Ctx(nc, sems)
    with nc.Block() as block:
        @block.tensor
        def _(eng):
            ctx.run("tensor", eng, emit)

        @block.scalar
        def _(eng):
            ctx.run("scalar", eng, emit)

        @block.vector
        def _(eng):
            ctx.run("vector", eng, emit)

        @block.gpsimd
        def _(eng):
            ctx.run("gpsimd", eng, emit)

        @block.sync
        def _(eng):
            ctx.run("sync", eng, emit)

    return nc


def _constants():
    bf = ml_dtypes.bfloat16
    ident = np.eye(128, dtype=bf)
    rp = np.zeros((128, 128), dtype=np.float32)
    for i in range(64):
        rp[2 * i + 1, 2 * i] = 1.0
        rp[2 * i, 2 * i + 1] = 1.0
    freqs = (1.0 / 10000.0) ** (np.arange(0, DH, 2, dtype=np.float64) / DH)
    t = np.arange(T_LEN, dtype=np.float64)
    ang = np.outer(t, freqs)
    cosv = np.cos(ang)
    sinv = np.sin(ang)
    rcos = np.zeros((128, T_LEN), dtype=np.float32)
    rsin = np.zeros((128, T_LEN), dtype=np.float32)
    for f in range(128):
        i = (f % 64) // 2
        rcos[f] = cosv[:, i]
        rsin[f] = sinv[:, i] * (-1.0 if f % 2 == 0 else 1.0)
    cm = np.zeros((128, 896), dtype=np.float32)
    for p in range(128):
        cm[p, 384 + p:] = 1.0
    return (ident, rp.astype(bf), rcos.astype(bf), rsin.astype(bf), cm.astype(bf))


_NC_CACHE = None


def kernel(L0_attn_x, L0_attn_W_qkv, L0_attn_W_out, L0_ffn_gate_W, L0_ffn_gate_b,
           L0_ffn_wi_W, L0_ffn_wi_b, L0_ffn_wo_W, L0_ffn_wo_b,
           L0_ffn_lora_A, L0_ffn_lora_B):
    global _NC_CACHE
    x = np.asarray(L0_attn_x, dtype=np.float32)
    W_qkv = np.asarray(L0_attn_W_qkv, dtype=np.float32)
    W_out = np.asarray(L0_attn_W_out, dtype=np.float32)
    gate_W = np.asarray(L0_ffn_gate_W, dtype=np.float32)
    gate_b = np.asarray(L0_ffn_gate_b, dtype=np.float32)
    wi_W = np.asarray(L0_ffn_wi_W, dtype=np.float32)
    wi_b = np.asarray(L0_ffn_wi_b, dtype=np.float32)
    wo_W = np.asarray(L0_ffn_wo_W, dtype=np.float32)
    wo_b = np.asarray(L0_ffn_wo_b, dtype=np.float32)
    lora_A = np.asarray(L0_ffn_lora_A, dtype=np.float32)
    lora_B = np.asarray(L0_ffn_lora_B, dtype=np.float32)

    wqkvT = np.ascontiguousarray(W_qkv.T)
    woutT = np.ascontiguousarray(W_out.T)
    gA = np.ascontiguousarray(
        np.concatenate([gate_W.T, lora_A.reshape(32, 1024).T], axis=1))
    brow = np.concatenate([gate_b, np.zeros(32, np.float32)]).reshape(1, 40)
    bcatT = np.ascontiguousarray(lora_B.transpose(0, 2, 1).reshape(32, FF))
    wiT = np.ascontiguousarray(wi_W.T)
    wib2 = np.ascontiguousarray(wi_b.reshape(32, 128).T)
    woT = np.ascontiguousarray(wo_W.T)
    wob2 = np.ascontiguousarray(wo_b.reshape(8, 128).T)
    ident, rp, rcos, rsin, cmask = _constants()

    if _NC_CACHE is None:
        _NC_CACHE = build_program()
    nc = _NC_CACHE

    shared = dict(wqkvT=wqkvT, woutT=woutT, gA=gA, brow=brow, bcatT=bcatT,
                  wiT=wiT, wib2=wib2, woT=woT, wob2=wob2,
                  idb=ident, rp=rp, rcos=rcos, rsin=rsin, cmask=cmask)
    in_maps = [dict(shared, xT=np.ascontiguousarray(x[b].T)) for b in range(NCORES)]
    res = run_bass_kernel_spmd(nc, in_maps, core_ids=list(range(NCORES)))
    out = np.stack([np.ascontiguousarray(r["outT"].T) for r in res.results], axis=0)
    return out


# version 24
# speedup vs baseline: 1.0038x; 1.0038x over previous
import numpy as np
import ml_dtypes
import concourse.bass as bass
import concourse.mybir as mybir
from concourse.bass_utils import run_bass_kernel_spmd

F32 = mybir.dt.float32
BF16 = mybir.dt.bfloat16
AF = mybir.ActivationFunctionType
ALU = mybir.AluOpType
X = mybir.AxisListType.X

T_LEN = 2048
D = 1024
NH = 16
DH = 64
FF = 4096
NCORES = 8


class Ctx:

    def __init__(self, nc, sems):
        self.nc = nc
        self.sems = sems
        self.active = None
        self.eng = None
        self.cnt = None
        self.last = None

    def run(self, name, eng, emit_fn):
        self.active, self.eng = name, eng
        self.cnt = {k: 0 for k in self.sems}
        self.last = None
        emit_fn(self)

    def op(self, ename, method, *a, **k):
        if self.active == ename:
            self.last = getattr(self.eng, method)(*a, **k)
        else:
            self.last = None
        return self

    def inc(self, sem, n=1):
        self.cnt[sem] += n
        if self.last is not None:
            self.last.then_inc(self.sems[sem], n)
        self.last = None
        return self.cnt[sem]

    def wait(self, ename, sem, val=None):
        v = self.cnt[sem] if val is None else val
        if v <= 0:
            return
        if self.active == ename:
            self.eng.wait_ge(self.sems[sem], v)

    def now(self, sem):
        return self.cnt[sem]


def build_program(debug=False):
    nc = bass.Bass("TRN2", target_bir_lowering=False)

    dt_in = {}
    def din(name, shape, dtype=F32):
        dt_in[name] = nc.dram_tensor(name, shape, dtype, kind="ExternalInput")
        return dt_in[name]

    xT = din("xT", [D, T_LEN])
    wqkvT = din("wqkvT", [D, 3 * D])
    woutT = din("woutT", [D, D])
    gAd = din("gA", [D, 40])
    browd = din("brow", [1, 40])
    bcatTd = din("bcatT", [32, FF])
    wiTd = din("wiT", [D, FF])
    wib2d = din("wib2", [128, 32])
    woTd = din("woT", [FF, D])
    wob2d = din("wob2", [128, 8])
    idbd = din("idb", [128, 128], BF16)
    rpd = din("rp", [128, 128], BF16)
    rcosd = din("rcos", [128, T_LEN], BF16)
    rsind = din("rsin", [128, T_LEN], BF16)
    cmaskd = din("cmask", [128, 896], BF16)

    outT = nc.dram_tensor("outT", [D, T_LEN], F32, kind="ExternalOutput")
    dbg = {}
    if debug:
        dbg["QB0"] = nc.dram_tensor("dQB0", [128, 16384], BF16, kind="ExternalOutput")
        dbg["KB0"] = nc.dram_tensor("dKB0", [128, 16384], BF16, kind="ExternalOutput")
        dbg["VB0"] = nc.dram_tensor("dVB0", [128, 16640], BF16, kind="ExternalOutput")
        dbg["OSB"] = nc.dram_tensor("dOSB", [128, 4096], BF16, kind="ExternalOutput")
        dbg["Z1"] = nc.dram_tensor("dZ1", [128, 4096], BF16, kind="ExternalOutput")
        dbg["Y1"] = nc.dram_tensor("dY1", [128, 16384], BF16, kind="ExternalOutput")
        dbg["CT"] = nc.dram_tensor("dCT", [32, 2048], BF16, kind="ExternalOutput")
        dbg["INT"] = nc.dram_tensor("dINT", [128, 16384], BF16, kind="ExternalOutput")
        dbg["Z2"] = nc.dram_tensor("dZ2", [128, 4096], BF16, kind="ExternalOutput")
    wiS = nc.dram_tensor("wiS", [128, 8 * FF], BF16, kind="Internal")
    woS = nc.dram_tensor("woS", [128, 32 * D], BF16, kind="Internal")

    A = nc.alloc_sbuf_tensor
    XB = A("XB", [128, 16384], BF16)
    QB = A("QB", [128, 16384], BF16)
    KB = A("KB", [128, 16384], BF16)
    VB = A("VB", [128, 16640], BF16)
    AR = A("AR", [128, 18432], BF16)
    rcos = A("rcos_s", [128, T_LEN], BF16).ap()
    rsin = A("rsin_s", [128, T_LEN], BF16).ap()
    cmask = A("cmask_s", [128, 896], BF16).ap()
    idb = A("idb_s", [128, 128], BF16).ap()
    rp = A("rp_s", [128, 128], BF16).ap()
    z1 = A("z1", [128, 4096], BF16).ap()
    sq = A("sq", [128, 4096], BF16)
    rows = A("rows", [1, 2048], F32).ap()
    gA = A("gAs", [128, 320], BF16).ap()
    brow = A("brows", [1, 40], BF16).ap()
    wib2 = A("wib2s", [128, 32], F32).ap()
    wob2 = A("wob2s", [128, 8], F32).ap()
    onescol = A("onescol", [128, 1], BF16).ap()
    eps = A("eps", [1, 1], F32).ap()
    onesrow_b = A("onesrow_b", [1, 128], BF16).ap()
    onesrow_f = A("onesrow_f", [1, 128], F32).ap()
    aux = A("aux", [128, 16], F32).ap()
    den = A("den", [128, 8], F32).ap()
    mask_f = A("mask_f", [128, 8], F32).ap()
    mask_b = A("mask_b", [128, 8], BF16).ap()
    c_st = A("c_st", [128, 32], BF16).ap()
    cT = A("cT", [32, 2048], BF16).ap()
    p3 = A("p3", [128, 1024], BF16).ap()

    XBa, QBa, KBa, VBa, ARa, sqa = XB.ap(), QB.ap(), KB.ap(), VB.ap(), AR.ap(), sq.ap()
    VBf = VB.bitcast(F32)

    wq_st = ARa[:, 0:12288].rearrange("p (dc o) -> p dc o", o=1536)
    wout_sb = ARa[:, 0:8192]
    o_sb = ARa[:, 8192:12288]
    oT_sb = ARa[:, 12288:16384]
    p_sb = [ARa[:, 16384:17408], ARa[:, 17408:18432], p3]
    wi_buf = [KBa[:, 0:4096], KBa[:, 4096:8192]]
    bcat = KBa[0:32, 8192:12288]

    z2 = VBa[:, 0:4096]
    y2 = VBf[:, 2048:6144]
    wo_buf = [ARa[:, 0:8192], ARa[:, 8192:16384]]

    praw = [sqa[:, 0:512], sqa[:, 512:1024]]
    swp = [sqa[:, 1024:1536], sqa[:, 1536:2048]]
    t1b = [sqa[:, 2048:2560], sqa[:, 2560:3072]]
    t2b = [sqa[:, 3072:3584], sqa[:, 3584:4096]]

    r_mean, r_var, r_tmp, r_nmr = (rows[:, 0:512], rows[:, 512:1024],
                                   rows[:, 1024:1536], rows[:, 1536:2048])

    P = nc.alloc_psum_tensor
    psA = P("psA", [128, 1024], F32).ap()
    psB = P("psB", [128, 1024], F32).ap()
    psC = P("psC", [128, 1024], F32).ap()
    psD = P("psD", [128, 512], F32).ap()
    psE = P("psE", [128, 1024], BF16).ap()
    rot4 = [psA[:, 0:512], psA[:, 512:1024], psB[:, 0:512], psB[:, 512:1024]]
    psC_h = [psC[:, 0:512], psC[:, 512:1024]]

    sems = {n: nc.alloc_semaphore(n) for n in
            ["ld", "lg", "kql", "vql", "wol", "kq2", "vq2", "wo2", "wis", "wos",
             "bct", "li0", "li1", "lo0", "lo1", "pe", "ac", "ve", "po", "st"]}

    def emit(c: Ctx):
        Tn, Sc, Ve, Gp, Sy = "tensor", "scalar", "vector", "gpsimd", "sync"

        c.op(Sy, "dma_start", idb, idbd.ap()); c.inc("ld", 16)
        c.op(Sy, "dma_start", rp, rpd.ap()); c.inc("ld", 16)
        c.op(Sy, "dma_start", rcos, rcosd.ap()); c.inc("ld", 16)
        c.op(Sy, "dma_start", rsin, rsind.ap()); c.inc("ld", 16)
        c.op(Sy, "dma_start", cmask, cmaskd.ap()); c.inc("ld", 16)
        c.op(Sy, "dma_start", wib2, wib2d.ap()); c.inc("ld", 16)
        c.op(Sy, "dma_start", wob2, wob2d.ap()); c.inc("ld", 16)

        c.op(Gp, "dma_start", XBa.rearrange("p (dc t) -> p dc t", t=T_LEN),
             xT.rearrange("(dc p) t -> p dc t", p=128)); c.inc("lg", 16)
        wq_r = wqkvT.rearrange("(dc p) o -> p dc o", p=128)
        c.op(Gp, "dma_start", wq_st[:, :, 0:512], wq_r[:, :, 0:512])
        WQ_P1 = ("kql", c.inc("kql", 16))
        c.op(Gp, "dma_start", wq_st[:, :, 512:1024], wq_r[:, :, 512:1024])
        WQ_P2 = ("vql", c.inc("vql", 16))
        c.op(Gp, "dma_start", wq_st[:, :, 1024:1536], wq_r[:, :, 1024:1536])
        WQ_P3 = ("wol", c.inc("wol", 16))
        c.op(Gp, "dma_start", gA.rearrange("p (dc e) -> p dc e", e=40),
             gAd.rearrange("(dc p) e -> p dc e", p=128)); c.inc("lg", 16)
        c.op(Gp, "dma_start", brow, browd.ap()); c.inc("lg", 16)
        LD_ALL = c.now("ld")
        LG_ALL = c.now("lg")

        c.op(Ve, "memset",
             VBa.rearrange("p (kt h x) -> p kt h x", h=16, x=65)[:, :, :, 64:65], 1.0)
        c.inc("ve")
        c.op(Ve, "memset", onescol, 1.0); c.inc("ve")
        c.op(Ve, "memset", onesrow_b, 1.0); c.inc("ve")
        c.op(Ve, "memset", onesrow_f, 1.0); c.inc("ve")
        c.op(Ve, "memset", eps, 1e-5); c.inc("ve")
        MEMSET_DONE = c.now("ve")

        slot_free_ac = [("ac", 0), ("ac", 0), ("ac", 0), ("ac", 0)]
        pc_free = [0, 0]
        praw_guard = [0, 0]
        po_guard = [0, 0]
        idx = 0
        QKV_A_MARK = None
        pend = []
        t1_guard = [0, 0]
        add_guard = [0, 0]

        def emit_rope(item):
            idx_, oc_, tg_, ev_praw_ = item
            pr_ = idx_ % 2
            h2 = idx_ % 2
            c.wait(Tn, "ac", ev_praw_)
            c.wait(Tn, "ac", pc_free[h2])
            c.op(Tn, "matmul", psC_h[h2], rp, praw[pr_], start=True, stop=True)
            pe_r = c.inc("pe")
            praw_guard[pr_] = pe_r
            c.wait(Sc, "pe", pe_r)
            c.wait(Sc, "po", po_guard[h2])
            c.op(Sc, "copy", swp[h2], psC_h[h2])
            ev_swp = c.inc("ac")
            pc_free[h2] = ev_swp
            c.wait(Ve, "ac", ev_praw_)
            c.op(Ve, "tensor_mul", t1b[h2], praw[pr_],
                 rcos[:, tg_ * 512:(tg_ + 1) * 512])
            ev_t1 = c.inc("ve")
            c.wait(Gp, "ac", ev_swp)
            c.wait(Gp, "ve", add_guard[h2])
            c.op(Gp, "tensor_mul", t2b[h2], swp[h2],
                 rsin[:, tg_ * 512:(tg_ + 1) * 512])
            po_t2 = c.inc("po")
            po_guard[h2] = po_t2
            dst = (QBa[:, oc_ * 2048 + tg_ * 512: oc_ * 2048 + (tg_ + 1) * 512]
                   if oc_ < 8 else
                   KBa[:, (oc_ - 8) * 2048 + tg_ * 512: (oc_ - 8) * 2048 + (tg_ + 1) * 512])
            c.wait(Ve, "ve", ev_t1)
            c.wait(Ve, "po", po_t2)
            c.op(Ve, "tensor_add", dst, t1b[h2], t2b[h2])
            add_guard[h2] = c.inc("ve")
            t1_guard[h2] = ev_t1

        LW_K = LW_V = None
        for oc in range(16):
            half2 = oc >= 12
            for tg in range(4):
                r = idx % 4
                c.wait(Tn, "ld", LD_ALL)
                c.wait(Tn, "lg", LG_ALL)
                if oc < 4:
                    c.wait(Tn, *WQ_P1)
                elif oc < 8:
                    c.wait(Tn, *WQ_P2)
                elif oc < 12:
                    c.wait(Tn, *WQ_P3)
                else:
                    c.wait(Tn, *LW_K)
                c.wait(Tn, slot_free_ac[r][0], slot_free_ac[r][1])
                for dc in range(8):
                    if oc < 8:
                        ocol = oc * 128
                    elif oc < 12:
                        ocol = 1024 + (oc - 8) * 128
                    else:
                        ocol = (oc - 12) * 128
                    c.op(Tn, "matmul", rot4[r],
                         wq_st[:, dc, ocol:ocol + 128],
                         XBa[:, dc * 2048 + tg * 512: dc * 2048 + (tg + 1) * 512],
                         start=(dc == 0), stop=(dc == 7))
                pe_proj = c.inc("pe")
                pr = idx % 2
                c.wait(Sc, "pe", pe_proj)
                c.wait(Sc, "pe", praw_guard[pr])
                c.wait(Sc, "ve", t1_guard[pr])
                c.op(Sc, "copy", praw[pr], rot4[r])
                ev_praw = c.inc("ac")
                slot_free_ac[r] = ("ac", ev_praw)
                pend.append((idx, oc, tg, ev_praw))
                if len(pend) >= 2:
                    emit_rope(pend.pop(0))
                idx += 1
            if oc == 3:
                c.wait(Gp, "pe", c.now("pe"))
                c.op(Gp, "dma_start", wq_st[:, :, 0:512],
                     wqkvT.rearrange("(dc p) o -> p dc o", p=128)[:, :, 1536:2048])
                LW_K = ("kq2", c.inc("kq2", 16))
            if oc == 11:
                for item in pend:
                    emit_rope(item)
                pend = []
                c.wait(Gp, "pe", c.now("pe"))
                c.op(Gp, "dma_start", wq_st[:, :, 512:1536],
                     wqkvT.rearrange("(dc p) o -> p dc o", p=128)[:, :, 2048:3072])
                LW_V = ("vq2", c.inc("vq2", 16))

        for item in pend:
            emit_rope(item)
        pend = []

        vidx = 0
        for tt in range(16):
            for sl in range(2):
                r = vidx % 4
                c.wait(Tn, slot_free_ac[r][0], slot_free_ac[r][1])
                c.wait(Tn, *LW_V)
                for dc in range(8):
                    c.op(Tn, "matmul", rot4[r],
                         XBa[:, dc * 2048 + tt * 128: dc * 2048 + (tt + 1) * 128],
                         wq_st[:, dc, 512 + sl * 512: 512 + (sl + 1) * 512],
                         start=(dc == 0), stop=(dc == 7))
                pe_v = c.inc("pe")
                c.wait(Ve, "pe", pe_v)
                c.wait(Ve, "ve", MEMSET_DONE)
                base = tt * 1040 + sl * 8 * 65
                c.op(Ve, "tensor_copy",
                     VBa[:, base:base + 520].rearrange("p (h x) -> p h x", x=65)[:, :, 0:64],
                     rot4[r].rearrange("p (h x) -> p h x", x=64))
                slot_free_ac[r] = ("ve", c.inc("ve"))
                vidx += 1
        V_DONE_PE = c.now("pe")
        ROPE_V_DONE = c.now("ve")
        ROPE_V_AC = c.now("ac")
        if debug:
            c.wait(Sy, "ve", ROPE_V_DONE)
            c.op(Sy, "dma_start", dbg["QB0"].ap(), QBa); c.inc("st", 16)
            c.op(Sy, "dma_start", dbg["KB0"].ap(), KBa); c.inc("st", 16)
            c.op(Sy, "dma_start", dbg["VB0"].ap(), VBa); c.inc("st", 16)

        c.wait(Gp, "pe", V_DONE_PE)
        c.op(Gp, "dma_start", wout_sb.rearrange("p (dc o) -> p dc o", o=1024),
             woutT.rearrange("(dc p) o -> p dc o", p=128))
        WOUT_LW = c.inc("wo2", 16)
        c.op(Gp, "dma_start", wiS.rearrange("p (dc f) -> p dc f", f=FF),
             wiTd.rearrange("(dc p) f -> p dc f", p=128))
        WIS_LW = c.inc("wis", 16)
        c.op(Gp, "dma_start", woS.rearrange("p (fc d) -> p fc d", d=D),
             woTd.rearrange("(fc p) d -> p fc d", p=128))
        WOS_LW = c.inc("wos", 16)

        av_prev_free = 0
        exp_free = [0, 0]
        state = {"psD_ve": 0, "psE_ve": 0, "sq_ve": 0}
        ve_y = [0, 0, 0, 0]
        ar_mark = [0, 0, 0, 0]
        z1_mark = [0, 0, 0, 0]
        ct_mark = [0, 0, 0, 0]
        tp_mark = [0, 0, 0, 0]
        av_prev_pe = 0

        def tail_gen(g, O_D):
            for grp in range(4):
                c.wait(Tn, "ve", state["psE_ve"])
                if grp == 0:
                    c.wait(Tn, "ve", O_D)
                for dc in range(8):
                    c.op(Tn, "transpose", psE[:, dc * 128:(dc + 1) * 128],
                         o_sb[:, grp * 1024 + dc * 128: grp * 1024 + (dc + 1) * 128],
                         idb)
                    if dc == 3:
                        yield
                pe_t = c.inc("pe")
                c.wait(Ve, "pe", pe_t)
                for dc in range(8):
                    c.op(Ve, "tensor_copy",
                         oT_sb[:, dc * 512 + grp * 128: dc * 512 + (grp + 1) * 128],
                         psE[:, dc * 128:(dc + 1) * 128])
                state["psE_ve"] = c.inc("ve")
                yield
            tp_mark[g] = state["psE_ve"]
            OT_DONE = state["psE_ve"]
            sq_guard = state["sq_ve"]
            for doc in range(8):
                c.wait(Tn, "wo2", WOUT_LW)
                c.wait(Tn, "ve", state["psD_ve"])
                if doc == 0:
                    c.wait(Tn, "ve", OT_DONE)
                for dmc in range(8):
                    c.op(Tn, "matmul", psD,
                         wout_sb[:, dmc * 1024 + doc * 128: dmc * 1024 + (doc + 1) * 128],
                         oT_sb[:, dmc * 512:(dmc + 1) * 512],
                         start=(dmc == 0), stop=(dmc == 7))
                    if dmc == 3:
                        yield
                pe_op = c.inc("pe")
                c.wait(Ve, "pe", pe_op)
                c.wait(Ve, "ve", sq_guard)
                if debug and g == 1 and doc == 0:
                    c.wait(Ve, "st", c.now("st"))
                c.op(Ve, "tensor_add",
                     z1[:, doc * 512:(doc + 1) * 512], psD,
                     XBa[:, doc * 2048 + g * 512: doc * 2048 + (g + 1) * 512])
                state["psD_ve"] = c.inc("ve")
                yield
            VE_Z = c.now("ve")
            z1_mark[g] = VE_Z
            ar_mark[g] = c.now("pe")
            if debug and g == 0:
                c.wait(Sy, "ve", VE_Z)
                c.op(Sy, "dma_start", dbg["Z1"].ap(), z1); c.inc("st", 16)
            c.wait(Ve, "ve", VE_Z)
            c.op(Ve, "tensor_mul", sqa[:], z1, z1)
            ac_q = c.inc("ve")
            state["sq_ve"] = ac_q
            c.wait(Tn, "ve", VE_Z)
            c.wait(Tn, "ve", state["psD_ve"])
            for dc in range(8):
                c.op(Tn, "matmul", psD[0:1, :], onescol,
                     z1[:, dc * 512:(dc + 1) * 512], start=(dc == 0), stop=(dc == 7))
            pe_s1 = c.inc("pe")
            c.wait(Ve, "pe", pe_s1)
            c.op(Ve, "tensor_scalar", r_mean, psD[0:1, :], 1.0 / D, None, ALU.mult)
            state["psD_ve"] = c.inc("ve")
            yield
            yield
            c.wait(Tn, "ve", ac_q)
            c.wait(Tn, "ve", state["psD_ve"])
            for dc in range(8):
                c.op(Tn, "matmul", psD[0:1, :], onescol,
                     sqa[:, dc * 512:(dc + 1) * 512], start=(dc == 0), stop=(dc == 7))
            pe_s2 = c.inc("pe")
            c.wait(Ve, "pe", pe_s2)
            c.op(Ve, "tensor_scalar", r_var, psD[0:1, :], 1.0 / D, None, ALU.mult)
            v1 = c.inc("ve")
            state["psD_ve"] = v1
            c.wait(Ve, "ve", v1)
            c.op(Ve, "tensor_mul", r_tmp, r_mean, r_mean)
            v2 = c.inc("ve")
            c.wait(Ve, "ve", v2)
            c.op(Ve, "tensor_sub", r_var, r_var, r_tmp)
            ve_var = c.inc("ve")
            c.wait(Sc, "ve", ve_var)
            c.op(Sc, "activation", r_tmp, r_var, AF.Ln, bias=eps)
            a1 = c.inc("ac")
            c.wait(Sc, "ac", a1)
            c.op(Sc, "activation", r_tmp, r_tmp, AF.Exp, bias=0.0, scale=-0.5)
            ac_rs = c.inc("ac")
            c.wait(Ve, "ac", ac_rs)
            c.op(Ve, "scalar_tensor_tensor", r_nmr, r_mean, -1.0, r_tmp,
                 ALU.mult, ALU.mult)
            ve_nm = c.inc("ve")
            yield
            yield
            yield
            c.wait(Tn, "ve", ve_nm)
            c.wait(Tn, "ve", state["psD_ve"])
            c.op(Tn, "matmul", psD, onesrow_f, r_tmp, start=True, stop=True)
            pe_b1 = c.inc("pe")
            c.wait(Ve, "pe", pe_b1)
            for dc in range(8):
                c.op(Ve, "tensor_mul", sqa[:, dc * 512:(dc + 1) * 512],
                     z1[:, dc * 512:(dc + 1) * 512], psD)
            state["psD_ve"] = c.inc("ve")
            ve_t1m = state["psD_ve"]
            yield
            c.wait(Tn, "ve", state["psD_ve"])
            c.op(Tn, "matmul", psD, onesrow_f, r_nmr, start=True, stop=True)
            pe_b2 = c.inc("pe")
            c.wait(Ve, "pe", pe_b2)
            c.wait(Ve, "ve", ve_t1m)
            for dc in range(8):
                c.op(Ve, "tensor_add",
                     QBa[:, dc * 2048 + g * 512: dc * 2048 + (g + 1) * 512],
                     sqa[:, dc * 512:(dc + 1) * 512], psD)
            ve_y[g] = c.inc("ve")
            state["psD_ve"] = ve_y[g]
            yield
            pe_ct_prev = 0
            for tt in range(4):
                c.wait(Tn, "ve", ve_y[g])
                c.wait(Tn, "ve", state["psD_ve"])
                for dc in range(8):
                    c.op(Tn, "matmul", psD[:, 0:40],
                         QBa[:, dc * 2048 + g * 512 + tt * 128:
                             dc * 2048 + g * 512 + (tt + 1) * 128],
                         gA[:, dc * 40:(dc + 1) * 40],
                         start=(dc == 0), stop=False, skip_group_check=True)
                c.op(Tn, "matmul", psD[:, 0:40], onesrow_b, brow,
                     start=False, stop=True, skip_group_check=True)
                pe_g = c.inc("pe")
                c.wait(Ve, "pe", pe_g)
                c.op(Ve, "tensor_copy", aux[:, 0:8], psD[:, 0:8])
                s1 = c.inc("ve"); c.wait(Ve, "ve", s1)
                c.op(Ve, "reduce_max", aux[:, 8:9], aux[:, 0:8], axis=X)
                s2 = c.inc("ve"); c.wait(Ve, "ve", s2)
                c.op(Ve, "tensor_scalar", mask_f, aux[:, 0:8], aux[:, 8:9], 0.0,
                     ALU.is_ge, ALU.add, accum_out=aux[:, 9:10])
                s3 = c.inc("ve"); c.wait(Ve, "ve", s3)
                c.op(Ve, "reciprocal", aux[:, 10:11], aux[:, 9:10])
                s4 = c.inc("ve"); c.wait(Ve, "ve", s4)
                c.op(Ve, "tensor_scalar", mask_b, mask_f, aux[:, 10:11], None, ALU.mult)
                s5 = c.inc("ve"); c.wait(Ve, "ve", s5)
                c.wait(Ve, "pe", pe_ct_prev)
                c.op(Ve, "tensor_tensor",
                     c_st.rearrange("p (e r) -> p e r", e=8),
                     psD[:, 8:40].rearrange("p (e r) -> p e r", e=8),
                     mask_b[:, :, None].broadcast_to([128, 8, 4]),
                     ALU.mult)
                ve_c = c.inc("ve")
                state["psD_ve"] = ve_c
                yield
                c.wait(Tn, "ve", ve_c)
                c.wait(Tn, "ve", state["psE_ve"])
                c.op(Tn, "transpose", psE[0:32, 0:128], c_st, idb)
                pe_ct = c.inc("pe")
                pe_ct_prev = pe_ct
                c.wait(Ve, "pe", pe_ct)
                c.op(Ve, "tensor_copy",
                     cT[:, g * 512 + tt * 128: g * 512 + (tt + 1) * 128],
                     psE[0:32, 0:128])
                state["psE_ve"] = c.inc("ve")
                ct_mark[g] = state["psE_ve"]
                yield

        tail = None
        pump = [0]
        for g in range(4):
            ktmax = 4 * g + 4
            if g == 0:
                c.wait(Tn, "ve", ROPE_V_DONE)
                c.wait(Tn, "ac", ROPE_V_AC)
            for cc in range(8):
                pe_s = {}
                ac_e = {}
                ve_m = {}
                av_done = {}

                def emit_av(kt):
                    pp = kt % 3
                    c.wait(Tn, "ac", ac_e[kt])
                    j = kt - 4 * g
                    if kt == 0:
                        c.wait(Tn, "ve", av_prev_free)
                    first_unmasked = 1 if (kt == 0 and j == 0) else 0
                    for masked in (False, True):
                        if masked:
                            if j < 0:
                                continue
                            c.wait(Tn, "ve", ve_m[kt])
                        for hl in (0, 1):
                            for qt in range(4):
                                if j >= 0 and qt < j:
                                    continue
                                if (qt == j) != masked:
                                    continue
                                col = hl * 512 + qt * 65
                                c.op(Tn, "matmul", psC[:, col: col + 65],
                                     p_sb[pp][:, hl * 512 + qt * 128:
                                               hl * 512 + (qt + 1) * 128],
                                     VBa[:, kt * 1040 + (2 * cc + hl) * 65:
                                         kt * 1040 + (2 * cc + hl) * 65 + 65],
                                     start=(kt == 0 and not masked
                                            and qt == first_unmasked),
                                     stop=(kt == ktmax - 1),
                                     skip_group_check=True)
                    av_done[kt] = c.inc("pe")

                for kt in range(ktmax):
                    pp2 = kt % 2
                    pp = kt % 3
                    psS = psA if pp2 == 0 else psB
                    c.wait(Tn, "ac", exp_free[pp2])
                    pump[0] += 1
                    for _ in range(2 if tail is None else 1):
                        c.op(Tn, "matmul", psS[:, 0:512], idb,
                             QBa[:, 0:512], start=True, stop=True)
                    for hl in (0, 1):
                        c.op(Tn, "matmul", psS[:, hl * 512:(hl + 1) * 512],
                             KBa[hl * 64:(hl + 1) * 64,
                                 cc * 2048 + kt * 128: cc * 2048 + (kt + 1) * 128],
                             QBa[hl * 64:(hl + 1) * 64,
                                 cc * 2048 + g * 512: cc * 2048 + (g + 1) * 512],
                             start=True, stop=True)
                    pe_s[kt] = c.inc("pe")
                    c.wait(Sc, "pe", pe_s[kt])
                    c.wait(Sc, "pe", av_done[kt - 3] if kt >= 3 else av_prev_pe)
                    jj = kt - 4 * g
                    if jj > 0:
                        c.op(Sc, "activation",
                             p_sb[pp].rearrange("p (h q) -> p h q", q=512)
                             [:, :, jj * 128:512],
                             psS.rearrange("p (h q) -> p h q", q=512)
                             [:, :, jj * 128:512],
                             AF.Exp, bias=0.0, scale=0.125)
                    else:
                        c.op(Sc, "activation", p_sb[pp], psS, AF.Exp,
                             bias=0.0, scale=0.125)
                    ac_e[kt] = c.inc("ac")
                    exp_free[pp2] = ac_e[kt]
                    if kt >= 4 * g:
                        j = kt - 4 * g
                        c.wait(Ve, "ac", ac_e[kt])
                        c.op(Ve, "tensor_tensor",
                             p_sb[pp].rearrange("p (h q) -> p h q", q=512)
                             [:, :, j * 128:(j + 1) * 128],
                             p_sb[pp].rearrange("p (h q) -> p h q", q=512)
                             [:, :, j * 128:(j + 1) * 128],
                             cmask[:, 384:512][:, None, :].broadcast_to([128, 2, 128]),
                             ALU.mult)
                        ve_m[kt] = c.inc("ve")
                    if tail is not None:
                        next(tail, None)
                    if kt >= 2:
                        emit_av(kt - 2)
                emit_av(ktmax - 2)
                emit_av(ktmax - 1)
                av_prev_pe = av_done[ktmax - 1]
                if g >= 1 and cc == 0:
                    if tail is not None:
                        while tp_mark[g - 1] == 0:
                            if next(tail, "END") == "END":
                                break
                c.wait(Ve, "pe", av_done[ktmax - 1])
                if cc == 0 and g >= 1:
                    c.wait(Ve, "ve", tp_mark[g - 1])
                    if debug:
                        c.wait(Ve, "st", c.now("st"))
                psC_v = (psC.rearrange("p (hl x) -> p hl x", hl=2)[:, :, 0:260]
                         .rearrange("p hl (qt d) -> p hl qt d", d=65))
                c.op(Ve, "reciprocal",
                     den.rearrange("p (hl qt) -> p hl qt", hl=2)[:, :, :, None],
                     psC_v[:, :, :, 64:65])
                ve_d = c.inc("ve")
                c.wait(Ve, "ve", ve_d)
                o_dst = (o_sb.rearrange("p (qt h2 d) -> p qt h2 d", h2=16, d=64)
                         [:, :, 2 * cc:2 * cc + 2, :].transpose([0, 2, 1, 3]))
                c.op(Ve, "tensor_tensor", o_dst,
                     psC_v[:, :, :, 0:64],
                     den.rearrange("p (hl qt) -> p hl qt", hl=2)[:, :, :, None]
                     .broadcast_to([128, 2, 4, 64]),
                     ALU.mult)
                av_prev_free = c.inc("ve")
            if debug and g == 0:
                c.wait(Sy, "ve", av_prev_free)
                c.op(Sy, "dma_start", dbg["OSB"].ap(), o_sb); c.inc("st", 16)
            if tail is not None:
                for _ in tail:
                    pass
            tail = tail_gen(g, c.now("ve"))

        c.wait(Gp, "pe", c.now("pe"))
        c.op(Gp, "dma_start", bcat, bcatTd.ap())
        BCAT_LW = c.inc("bct", 16)

        wi_buf_free = [c.now("pe"), c.now("pe")]
        wo_buf_free = [c.now("pe"), c.now("pe")]
        lw_wi = {}
        lw_wo = {}
        nxt_lw0 = []
        relu_free = [exp_free[0], exp_free[0], exp_free[1], exp_free[1]]
        z2ev = [av_prev_free, av_prev_free]
        ve_y2 = 0
        ST_PREV = 0
        wiS_v = wiS.rearrange("p (dc f) -> p dc f", f=FF)
        woS_v = woS.rearrange("p (fc d) -> p fc d", d=D)
        st_box = [0]

        def ln2_gen(g, VE_Z2):
            c.wait(Ve, "ve", VE_Z2)
            c.op(Ve, "tensor_mul", sqa[:], z2, z2)
            ve_q2 = c.inc("ve")
            c.wait(Tn, "ve", VE_Z2)
            c.wait(Tn, "ve", state["psD_ve"])
            for dc in range(8):
                c.op(Tn, "matmul", psD[0:1, :], onescol,
                     z2[:, dc * 512:(dc + 1) * 512], start=(dc == 0), stop=(dc == 7))
            pe_s1 = c.inc("pe")
            c.wait(Ve, "pe", pe_s1)
            c.op(Ve, "tensor_scalar", r_mean, psD[0:1, :], 1.0 / D, None, ALU.mult)
            state["psD_ve"] = c.inc("ve")
            yield
            c.wait(Tn, "ve", ve_q2)
            c.wait(Tn, "ve", state["psD_ve"])
            for dc in range(8):
                c.op(Tn, "matmul", psD[0:1, :], onescol,
                     sqa[:, dc * 512:(dc + 1) * 512], start=(dc == 0), stop=(dc == 7))
            pe_s2 = c.inc("pe")
            c.wait(Ve, "pe", pe_s2)
            c.op(Ve, "tensor_scalar", r_var, psD[0:1, :], 1.0 / D, None, ALU.mult)
            v1 = c.inc("ve")
            state["psD_ve"] = v1
            c.wait(Ve, "ve", v1)
            c.op(Ve, "tensor_mul", r_tmp, r_mean, r_mean)
            v2 = c.inc("ve")
            c.wait(Ve, "ve", v2)
            c.op(Ve, "tensor_sub", r_var, r_var, r_tmp)
            ve_var = c.inc("ve")
            c.wait(Sc, "ve", ve_var)
            c.op(Sc, "activation", r_tmp, r_var, AF.Ln, bias=eps)
            a1 = c.inc("ac")
            c.wait(Sc, "ac", a1)
            c.op(Sc, "activation", r_tmp, r_tmp, AF.Exp, bias=0.0, scale=-0.5)
            ac_rs = c.inc("ac")
            c.wait(Ve, "ac", ac_rs)
            c.op(Ve, "scalar_tensor_tensor", r_nmr, r_mean, -1.0, r_tmp,
                 ALU.mult, ALU.mult)
            ve_nm = c.inc("ve")
            yield
            yield
            c.wait(Tn, "ve", ve_nm)
            c.wait(Tn, "ve", state["psD_ve"])
            c.op(Tn, "matmul", psD, onesrow_f, r_tmp, start=True, stop=True)
            pe_b1 = c.inc("pe")
            c.wait(Ve, "pe", pe_b1)
            c.wait(Ve, "st", st_box[0])
            for dc in range(8):
                c.op(Ve, "tensor_mul", sqa[:, dc * 512:(dc + 1) * 512],
                     z2[:, dc * 512:(dc + 1) * 512], psD)
            ve_t2 = c.inc("ve")
            state["psD_ve"] = ve_t2
            yield
            c.wait(Tn, "ve", state["psD_ve"])
            c.op(Tn, "matmul", psD, onesrow_f, r_nmr, start=True, stop=True)
            pe_b2 = c.inc("pe")
            c.wait(Ve, "pe", pe_b2)
            c.wait(Ve, "ve", ve_t2)
            for dc in range(8):
                c.op(Ve, "tensor_add", y2[:, dc * 512:(dc + 1) * 512],
                     sqa[:, dc * 512:(dc + 1) * 512], psD)
            ve_y2 = c.inc("ve")
            state["psD_ve"] = ve_y2
            c.wait(Sy, "ve", ve_y2)
            c.op(Sy, "dma_start",
                 outT.rearrange("(dc p) t -> p dc t", p=128)[:, :, g * 512:(g + 1) * 512],
                 y2.rearrange("p (dc t) -> p dc t", t=512))
            st_box[0] = c.inc("st", 16)
            yield

        ln2 = None
        for g in range(4):
            if g == 0:
                for e8 in (0, 1):
                    c.wait(Sy, "pe", wi_buf_free[e8 % 2])
                    c.wait(Sy, "wis", WIS_LW)
                    c.op(Sy, "dma_start",
                         wi_buf[e8 % 2].rearrange("p (dc f) -> p dc f", f=512),
                         wiS_v[:, :, e8 * 512:(e8 + 1) * 512])
                    lw_wi[e8] = ("li%d" % (e8 % 2), c.inc("li%d" % (e8 % 2), 16))
            if debug and g == 1:
                c.wait(Sc, "st", c.now("st"))
            if tail is not None:
                while z1_mark[3] == 0:
                    if next(tail, "END") == "END":
                        break
            for e8 in range(8):
                if e8 == 0 and g >= 1:
                    for _ in range(6):
                        c.op(Tn, "matmul", rot4[0], idb, XBa[:, 0:512],
                             start=True, stop=True)
                c.wait(Tn, *lw_wi[e8])
                c.wait(Tn, "bct", BCAT_LW)
                for fcl in range(4):
                    fc = e8 * 4 + fcl
                    r = fc % 4
                    c.wait(Tn, "ac", relu_free[r])
                    c.wait(Tn, "ve", ve_y[g])
                    c.wait(Tn, "ve", ct_mark[g])
                    for dc in range(8):
                        c.op(Tn, "matmul", rot4[r],
                             wi_buf[e8 % 2][:, dc * 512 + fcl * 128:
                                            dc * 512 + (fcl + 1) * 128],
                             QBa[:, dc * 2048 + g * 512: dc * 2048 + (g + 1) * 512],
                             start=(dc == 0), stop=False)
                    c.op(Tn, "matmul", rot4[r],
                         bcat[:, fc * 128:(fc + 1) * 128],
                         cT[:, g * 512:(g + 1) * 512],
                         start=False, stop=True)
                    pe_wi = c.inc("pe")
                    c.wait(Sc, "pe", pe_wi)
                    if g == 0 and fc == 0:
                        c.wait(Sc, "ve", z1_mark[3])
                    c.op(Sc, "activation", XBa[:, fc * 512:(fc + 1) * 512],
                         rot4[r], AF.Relu, bias=wib2[:, fc:fc + 1])
                    relu_free[r] = c.inc("ac")
                    if tail is not None:
                        next(tail, None)
                    if ln2 is not None:
                        next(ln2, None)
                wi_buf_free[e8 % 2] = c.now("pe")
                nxt = e8 + 2
                if nxt <= 7:
                    c.wait(Sy, "pe", wi_buf_free[nxt % 2])
                    c.op(Sy, "dma_start",
                         wi_buf[nxt % 2].rearrange("p (dc f) -> p dc f", f=512),
                         wiS_v[:, :, nxt * 512:(nxt + 1) * 512])
                    lw_wi[nxt] = ("li%d" % (nxt % 2), c.inc("li%d" % (nxt % 2), 16))
                elif g < 3:
                    ne = nxt - 8
                    c.wait(Sy, "pe", wi_buf_free[ne % 2])
                    c.op(Sy, "dma_start",
                         wi_buf[ne % 2].rearrange("p (dc f) -> p dc f", f=512),
                         wiS_v[:, :, ne * 512:(ne + 1) * 512])
                    lw_wi[ne] = ("li%d" % (ne % 2), c.inc("li%d" % (ne % 2), 16))
            if tail is not None:
                for _ in tail:
                    pass
                tail = None
                wo_buf_free[0] = c.now("pe")
                wo_buf_free[1] = c.now("pe")
            if ln2 is not None:
                for _ in ln2:
                    pass
                ln2 = None
                if debug:
                    c.wait(Sy, "ve", c.now("ve"))
                    c.op(Sy, "dma_start", dbg["Y1"].ap(), QBa); c.inc("st", 16)
                    c.op(Sy, "dma_start", dbg["CT"].ap(), cT); c.inc("st", 16)

            if debug and g == 0:
                c.wait(Sy, "ac", c.now("ac"))
                c.op(Sy, "dma_start", dbg["INT"].ap(), XBa); c.inc("st", 16)
            if g == 0:
                c.wait(Sy, "pe", wo_buf_free[0])
                c.wait(Sy, "pe", ar_mark[3])
                c.wait(Sy, "wos", WOS_LW)
                c.op(Sy, "dma_start",
                     wo_buf[0].rearrange("p (fc d) -> p fc d", d=256),
                     woS_v[:, :, 0:256])
                lw_wo[0] = ("lo0", c.inc("lo0", 16))
            for doc in range(8):
                dp = doc // 2
                if doc % 2 == 0 and dp + 1 <= 3:
                    c.wait(Sy, "pe", wo_buf_free[(dp + 1) % 2])
                    c.op(Sy, "dma_start",
                         wo_buf[(dp + 1) % 2].rearrange("p (fc d) -> p fc d", d=256),
                         woS_v[:, :, (dp + 1) * 256:(dp + 2) * 256])
                    lw_wo[dp + 1] = ("lo%d" % ((dp + 1) % 2), c.inc("lo%d" % ((dp + 1) % 2), 16))
                if doc == 6 and g < 3:
                    nxt_lw0.append(c.now("pe"))
                    c.wait(Sy, "pe", nxt_lw0[-1])
                    c.op(Sy, "dma_start",
                         wo_buf[0].rearrange("p (fc d) -> p fc d", d=256),
                         woS_v[:, :, 0:256])
                    lw_wo[0] = ("lo0", c.inc("lo0", 16))
                c.wait(Tn, *lw_wo[dp])
                h2 = doc % 2
                c.wait(Tn, "ve", z2ev[h2])
                if doc == 0:
                    for _ in range(6):
                        c.op(Tn, "matmul", psC_h[h2], idb, XBa[:, 0:512],
                             start=True, stop=True)
                c.wait(Tn, "ac", c.now("ac") if doc == 0 else 0)
                for fc in range(32):
                    c.op(Tn, "matmul", psC_h[h2],
                         wo_buf[dp % 2][:, fc * 256 + (doc % 2) * 128:
                                        fc * 256 + (doc % 2) * 128 + 128],
                         XBa[:, fc * 512:(fc + 1) * 512],
                         start=(fc == 0), stop=(fc == 31))
                pe_wo = c.inc("pe")
                if doc % 2 == 1:
                    wo_buf_free[dp % 2] = pe_wo
                c.wait(Ve, "pe", pe_wo)
                c.op(Ve, "scalar_tensor_tensor",
                     z2[:, doc * 512:(doc + 1) * 512],
                     psC_h[h2], wob2[:, doc:doc + 1],
                     QBa[:, doc * 2048 + g * 512: doc * 2048 + (g + 1) * 512],
                     ALU.add, ALU.add)
                z2ev[h2] = c.inc("ve")
            VE_Z2 = c.now("ve")
            if debug and g == 0:
                c.wait(Sy, "ve", VE_Z2)
                c.op(Sy, "dma_start", dbg["Z2"].ap(), z2); c.inc("st", 16)
            if debug and g == 1:
                c.wait(Ve, "st", c.now("st"))
            ln2 = ln2_gen(g, VE_Z2)
            if g == 3:
                for _ in ln2:
                    pass
                ln2 = None

        c.wait(Sy, "st", c.now("st"))

    ctx = Ctx(nc, sems)
    with nc.Block() as block:
        @block.tensor
        def _(eng):
            ctx.run("tensor", eng, emit)

        @block.scalar
        def _(eng):
            ctx.run("scalar", eng, emit)

        @block.vector
        def _(eng):
            ctx.run("vector", eng, emit)

        @block.gpsimd
        def _(eng):
            ctx.run("gpsimd", eng, emit)

        @block.sync
        def _(eng):
            ctx.run("sync", eng, emit)

    return nc


def _constants():
    bf = ml_dtypes.bfloat16
    ident = np.eye(128, dtype=bf)
    rp = np.zeros((128, 128), dtype=np.float32)
    for i in range(64):
        rp[2 * i + 1, 2 * i] = 1.0
        rp[2 * i, 2 * i + 1] = 1.0
    freqs = (1.0 / 10000.0) ** (np.arange(0, DH, 2, dtype=np.float64) / DH)
    t = np.arange(T_LEN, dtype=np.float64)
    ang = np.outer(t, freqs)
    cosv = np.cos(ang)
    sinv = np.sin(ang)
    rcos = np.zeros((128, T_LEN), dtype=np.float32)
    rsin = np.zeros((128, T_LEN), dtype=np.float32)
    for f in range(128):
        i = (f % 64) // 2
        rcos[f] = cosv[:, i]
        rsin[f] = sinv[:, i] * (-1.0 if f % 2 == 0 else 1.0)
    cm = np.zeros((128, 896), dtype=np.float32)
    for p in range(128):
        cm[p, 384 + p:] = 1.0
    return (ident, rp.astype(bf), rcos.astype(bf), rsin.astype(bf), cm.astype(bf))


_NC_CACHE = None


def kernel(L0_attn_x, L0_attn_W_qkv, L0_attn_W_out, L0_ffn_gate_W, L0_ffn_gate_b,
           L0_ffn_wi_W, L0_ffn_wi_b, L0_ffn_wo_W, L0_ffn_wo_b,
           L0_ffn_lora_A, L0_ffn_lora_B):
    global _NC_CACHE
    x = np.asarray(L0_attn_x, dtype=np.float32)
    W_qkv = np.asarray(L0_attn_W_qkv, dtype=np.float32)
    W_out = np.asarray(L0_attn_W_out, dtype=np.float32)
    gate_W = np.asarray(L0_ffn_gate_W, dtype=np.float32)
    gate_b = np.asarray(L0_ffn_gate_b, dtype=np.float32)
    wi_W = np.asarray(L0_ffn_wi_W, dtype=np.float32)
    wi_b = np.asarray(L0_ffn_wi_b, dtype=np.float32)
    wo_W = np.asarray(L0_ffn_wo_W, dtype=np.float32)
    wo_b = np.asarray(L0_ffn_wo_b, dtype=np.float32)
    lora_A = np.asarray(L0_ffn_lora_A, dtype=np.float32)
    lora_B = np.asarray(L0_ffn_lora_B, dtype=np.float32)

    wqkvT = np.ascontiguousarray(W_qkv.T)
    woutT = np.ascontiguousarray(W_out.T)
    gA = np.ascontiguousarray(
        np.concatenate([gate_W.T, lora_A.reshape(32, 1024).T], axis=1))
    brow = np.concatenate([gate_b, np.zeros(32, np.float32)]).reshape(1, 40)
    bcatT = np.ascontiguousarray(lora_B.transpose(0, 2, 1).reshape(32, FF))
    wiT = np.ascontiguousarray(wi_W.T)
    wib2 = np.ascontiguousarray(wi_b.reshape(32, 128).T)
    woT = np.ascontiguousarray(wo_W.T)
    wob2 = np.ascontiguousarray(wo_b.reshape(8, 128).T)
    ident, rp, rcos, rsin, cmask = _constants()

    if _NC_CACHE is None:
        _NC_CACHE = build_program()
    nc = _NC_CACHE

    shared = dict(wqkvT=wqkvT, woutT=woutT, gA=gA, brow=brow, bcatT=bcatT,
                  wiT=wiT, wib2=wib2, woT=woT, wob2=wob2,
                  idb=ident, rp=rp, rcos=rcos, rsin=rsin, cmask=cmask)
    in_maps = [dict(shared, xT=np.ascontiguousarray(x[b].T)) for b in range(NCORES)]
    res = run_bass_kernel_spmd(nc, in_maps, core_ids=list(range(NCORES)))
    out = np.stack([np.ascontiguousarray(r["outT"].T) for r in res.results], axis=0)
    return out


# version 25
# speedup vs baseline: 1.0260x; 1.0221x over previous
import numpy as np
import ml_dtypes
import concourse.bass as bass
import concourse.mybir as mybir
from concourse.bass_utils import run_bass_kernel_spmd

F32 = mybir.dt.float32
BF16 = mybir.dt.bfloat16
AF = mybir.ActivationFunctionType
ALU = mybir.AluOpType
X = mybir.AxisListType.X

T_LEN = 2048
D = 1024
NH = 16
DH = 64
FF = 4096
NCORES = 8


class Ctx:

    def __init__(self, nc, sems):
        self.nc = nc
        self.sems = sems
        self.active = None
        self.eng = None
        self.cnt = None
        self.last = None

    def run(self, name, eng, emit_fn):
        self.active, self.eng = name, eng
        self.cnt = {k: 0 for k in self.sems}
        self.last = None
        emit_fn(self)

    def op(self, ename, method, *a, **k):
        if self.active == ename:
            self.last = getattr(self.eng, method)(*a, **k)
        else:
            self.last = None
        return self

    def inc(self, sem, n=1):
        self.cnt[sem] += n
        if self.last is not None:
            self.last.then_inc(self.sems[sem], n)
        self.last = None
        return self.cnt[sem]

    def wait(self, ename, sem, val=None):
        v = self.cnt[sem] if val is None else val
        if v <= 0:
            return
        if self.active == ename:
            self.eng.wait_ge(self.sems[sem], v)

    def now(self, sem):
        return self.cnt[sem]


def build_program(debug=False):
    nc = bass.Bass("TRN2", target_bir_lowering=False)

    dt_in = {}
    def din(name, shape, dtype=F32):
        dt_in[name] = nc.dram_tensor(name, shape, dtype, kind="ExternalInput")
        return dt_in[name]

    xT = din("xT", [D, T_LEN])
    wqkvT = din("wqkvT", [D, 3 * D])
    woutT = din("woutT", [D, D])
    gAd = din("gA", [D, 40])
    browd = din("brow", [1, 40])
    bcatTd = din("bcatT", [32, FF])
    wiTd = din("wiT", [D, FF])
    wib2d = din("wib2", [128, 32])
    woTd = din("woT", [FF, D])
    wob2d = din("wob2", [128, 8])
    idbd = din("idb", [128, 128], BF16)
    rpd = din("rp", [128, 128], BF16)
    rcosd = din("rcos", [128, T_LEN], BF16)
    rsind = din("rsin", [128, T_LEN], BF16)
    cmaskd = din("cmask", [128, 896], BF16)

    outT = nc.dram_tensor("outT", [D, T_LEN], F32, kind="ExternalOutput")
    dbg = {}
    if debug:
        dbg["QB0"] = nc.dram_tensor("dQB0", [128, 16384], BF16, kind="ExternalOutput")
        dbg["KB0"] = nc.dram_tensor("dKB0", [128, 16384], BF16, kind="ExternalOutput")
        dbg["VB0"] = nc.dram_tensor("dVB0", [128, 16640], BF16, kind="ExternalOutput")
        dbg["OSB"] = nc.dram_tensor("dOSB", [128, 4096], BF16, kind="ExternalOutput")
        dbg["Z1"] = nc.dram_tensor("dZ1", [128, 4096], BF16, kind="ExternalOutput")
        dbg["Y1"] = nc.dram_tensor("dY1", [128, 16384], BF16, kind="ExternalOutput")
        dbg["CT"] = nc.dram_tensor("dCT", [32, 2048], BF16, kind="ExternalOutput")
        dbg["INT"] = nc.dram_tensor("dINT", [128, 16384], BF16, kind="ExternalOutput")
        dbg["Z2"] = nc.dram_tensor("dZ2", [128, 4096], BF16, kind="ExternalOutput")
    wiS = nc.dram_tensor("wiS", [128, 8 * FF], BF16, kind="Internal")
    woS = nc.dram_tensor("woS", [128, 32 * D], BF16, kind="Internal")

    A = nc.alloc_sbuf_tensor
    XB = A("XB", [128, 16384], BF16)
    QB = A("QB", [128, 16384], BF16)
    KB = A("KB", [128, 16384], BF16)
    VB = A("VB", [128, 16640], BF16)
    AR = A("AR", [128, 18432], BF16)
    rcos = A("rcos_s", [128, T_LEN], BF16).ap()
    rsin = A("rsin_s", [128, T_LEN], BF16).ap()
    cmask = A("cmask_s", [128, 896], BF16).ap()
    idb = A("idb_s", [128, 128], BF16).ap()
    rp = A("rp_s", [128, 128], BF16).ap()
    z1 = A("z1", [128, 4096], BF16).ap()
    sq = A("sq", [128, 4096], BF16)
    rows = A("rows", [1, 2048], F32).ap()
    gA = A("gAs", [128, 320], BF16).ap()
    brow = A("brows", [1, 40], BF16).ap()
    wib2 = A("wib2s", [128, 32], F32).ap()
    wob2 = A("wob2s", [128, 8], F32).ap()
    onescol = A("onescol", [128, 1], BF16).ap()
    eps = A("eps", [1, 1], F32).ap()
    onesrow_b = A("onesrow_b", [1, 128], BF16).ap()
    onesrow_f = A("onesrow_f", [1, 128], F32).ap()
    aux = A("aux", [128, 16], F32).ap()
    den = A("den", [128, 8], F32).ap()
    mask_f = A("mask_f", [128, 8], F32).ap()
    mask_b = A("mask_b", [128, 8], BF16).ap()
    c_st = A("c_st", [128, 32], BF16).ap()
    cT = A("cT", [32, 2048], BF16).ap()
    p3 = A("p3", [128, 1024], BF16).ap()

    XBa, QBa, KBa, VBa, ARa, sqa = XB.ap(), QB.ap(), KB.ap(), VB.ap(), AR.ap(), sq.ap()
    VBf = VB.bitcast(F32)

    wq_st = ARa[:, 0:12288].rearrange("p (dc o) -> p dc o", o=1536)
    wout_sb = ARa[:, 0:8192]
    o_sb = ARa[:, 8192:12288]
    oT_sb = ARa[:, 12288:16384]
    p_sb = [ARa[:, 16384:17408], ARa[:, 17408:18432], p3]
    wi_buf = [KBa[:, 0:4096], KBa[:, 4096:8192]]
    bcat = KBa[0:32, 8192:12288]

    z2 = VBa[:, 0:4096]
    y2 = VBf[:, 2048:6144]
    wo_buf = [ARa[:, 0:8192], ARa[:, 8192:16384]]

    praw = [sqa[:, 0:512], sqa[:, 512:1024]]
    swp = [sqa[:, 1024:1536], sqa[:, 1536:2048]]
    t1b = [sqa[:, 2048:2560], sqa[:, 2560:3072]]
    t2b = [sqa[:, 3072:3584], sqa[:, 3584:4096]]

    r_mean, r_var, r_tmp, r_nmr = (rows[:, 0:512], rows[:, 512:1024],
                                   rows[:, 1024:1536], rows[:, 1536:2048])

    P = nc.alloc_psum_tensor
    psA = P("psA", [128, 1024], F32).ap()
    psB = P("psB", [128, 1024], F32).ap()
    psC = P("psC", [128, 1024], F32).ap()
    psD = P("psD", [128, 512], F32).ap()
    psE = P("psE", [128, 1024], BF16).ap()
    rot4 = [psA[:, 0:512], psA[:, 512:1024], psB[:, 0:512], psB[:, 512:1024]]
    psC_h = [psC[:, 0:512], psC[:, 512:1024]]

    sems = {n: nc.alloc_semaphore(n) for n in
            ["ld", "lg", "kql", "vql", "wol", "kq2", "vq2", "wo2", "wis", "wos",
             "bct", "li0", "li1", "lo0", "lo1", "pe", "ac", "ve", "po", "st"]}

    def emit(c: Ctx):
        Tn, Sc, Ve, Gp, Sy = "tensor", "scalar", "vector", "gpsimd", "sync"

        c.op(Sy, "dma_start", idb, idbd.ap()); c.inc("ld", 16)
        c.op(Sy, "dma_start", rp, rpd.ap()); c.inc("ld", 16)
        c.op(Sy, "dma_start", rcos, rcosd.ap()); c.inc("ld", 16)
        c.op(Sy, "dma_start", rsin, rsind.ap()); c.inc("ld", 16)
        c.op(Sy, "dma_start", cmask, cmaskd.ap()); c.inc("ld", 16)
        c.op(Sy, "dma_start", wib2, wib2d.ap()); c.inc("ld", 16)
        c.op(Sy, "dma_start", wob2, wob2d.ap()); c.inc("ld", 16)

        c.op(Gp, "dma_start", XBa.rearrange("p (dc t) -> p dc t", t=T_LEN),
             xT.rearrange("(dc p) t -> p dc t", p=128)); c.inc("lg", 16)
        wq_r = wqkvT.rearrange("(dc p) o -> p dc o", p=128)
        c.op(Gp, "dma_start", wq_st[:, :, 0:512], wq_r[:, :, 0:512])
        WQ_P1 = ("kql", c.inc("kql", 16))
        c.op(Gp, "dma_start", wq_st[:, :, 512:1024], wq_r[:, :, 512:1024])
        WQ_P2 = ("vql", c.inc("vql", 16))
        c.op(Gp, "dma_start", wq_st[:, :, 1024:1536], wq_r[:, :, 1024:1536])
        WQ_P3 = ("wol", c.inc("wol", 16))
        c.op(Gp, "dma_start", gA.rearrange("p (dc e) -> p dc e", e=40),
             gAd.rearrange("(dc p) e -> p dc e", p=128)); c.inc("lg", 16)
        c.op(Gp, "dma_start", brow, browd.ap()); c.inc("lg", 16)
        LD_ALL = c.now("ld")
        LG_ALL = c.now("lg")

        c.op(Ve, "memset",
             VBa.rearrange("p (kt h x) -> p kt h x", h=16, x=65)[:, :, :, 64:65], 1.0)
        c.inc("ve")
        c.op(Ve, "memset", onescol, 1.0); c.inc("ve")
        c.op(Ve, "memset", onesrow_b, 1.0); c.inc("ve")
        c.op(Ve, "memset", onesrow_f, 1.0); c.inc("ve")
        c.op(Ve, "memset", eps, 1e-5); c.inc("ve")
        MEMSET_DONE = c.now("ve")

        slot_free_ac = [("ac", 0), ("ac", 0), ("ac", 0), ("ac", 0)]
        pc_free = [0, 0]
        praw_guard = [0, 0]
        po_guard = [0, 0]
        idx = 0
        QKV_A_MARK = None
        pend = []
        t1_guard = [0, 0]
        add_guard = [0, 0]

        def emit_rope(item):
            idx_, oc_, tg_, ev_praw_ = item
            pr_ = idx_ % 2
            h2 = idx_ % 2
            c.wait(Tn, "ac", ev_praw_)
            c.wait(Tn, "ac", pc_free[h2])
            c.op(Tn, "matmul", psC_h[h2], rp, praw[pr_], start=True, stop=True)
            pe_r = c.inc("pe")
            praw_guard[pr_] = pe_r
            c.wait(Sc, "pe", pe_r)
            c.wait(Sc, "po", po_guard[h2])
            c.op(Sc, "copy", swp[h2], psC_h[h2])
            ev_swp = c.inc("ac")
            pc_free[h2] = ev_swp
            c.wait(Ve, "ac", ev_praw_)
            c.op(Ve, "tensor_mul", t1b[h2], praw[pr_],
                 rcos[:, tg_ * 512:(tg_ + 1) * 512])
            ev_t1 = c.inc("ve")
            c.wait(Gp, "ac", ev_swp)
            c.wait(Gp, "ve", add_guard[h2])
            c.op(Gp, "tensor_mul", t2b[h2], swp[h2],
                 rsin[:, tg_ * 512:(tg_ + 1) * 512])
            po_t2 = c.inc("po")
            po_guard[h2] = po_t2
            dst = (QBa[:, oc_ * 2048 + tg_ * 512: oc_ * 2048 + (tg_ + 1) * 512]
                   if oc_ < 8 else
                   KBa[:, (oc_ - 8) * 2048 + tg_ * 512: (oc_ - 8) * 2048 + (tg_ + 1) * 512])
            c.wait(Ve, "ve", ev_t1)
            c.wait(Ve, "po", po_t2)
            c.op(Ve, "tensor_add", dst, t1b[h2], t2b[h2])
            add_guard[h2] = c.inc("ve")
            t1_guard[h2] = ev_t1

        LW_K = LW_V = None
        for oc in range(16):
            half2 = oc >= 12
            for tg in range(4):
                r = idx % 4
                c.wait(Tn, "ld", LD_ALL)
                c.wait(Tn, "lg", LG_ALL)
                if oc < 4:
                    c.wait(Tn, *WQ_P1)
                elif oc < 8:
                    c.wait(Tn, *WQ_P2)
                elif oc < 12:
                    c.wait(Tn, *WQ_P3)
                else:
                    c.wait(Tn, *LW_K)
                c.wait(Tn, slot_free_ac[r][0], slot_free_ac[r][1])
                for dc in range(8):
                    if oc < 8:
                        ocol = oc * 128
                    elif oc < 12:
                        ocol = 1024 + (oc - 8) * 128
                    else:
                        ocol = (oc - 12) * 128
                    c.op(Tn, "matmul", rot4[r],
                         wq_st[:, dc, ocol:ocol + 128],
                         XBa[:, dc * 2048 + tg * 512: dc * 2048 + (tg + 1) * 512],
                         start=(dc == 0), stop=(dc == 7))
                pe_proj = c.inc("pe")
                pr = idx % 2
                c.wait(Sc, "pe", pe_proj)
                c.wait(Sc, "pe", praw_guard[pr])
                c.wait(Sc, "ve", t1_guard[pr])
                c.op(Sc, "copy", praw[pr], rot4[r])
                ev_praw = c.inc("ac")
                slot_free_ac[r] = ("ac", ev_praw)
                pend.append((idx, oc, tg, ev_praw))
                if len(pend) >= 2:
                    emit_rope(pend.pop(0))
                idx += 1
            if oc == 3:
                c.wait(Gp, "pe", c.now("pe"))
                c.op(Gp, "dma_start", wq_st[:, :, 0:512],
                     wqkvT.rearrange("(dc p) o -> p dc o", p=128)[:, :, 1536:2048])
                LW_K = ("kq2", c.inc("kq2", 16))
            if oc == 11:
                for item in pend:
                    emit_rope(item)
                pend = []
                c.wait(Gp, "pe", c.now("pe"))
                c.op(Gp, "dma_start", wq_st[:, :, 512:1536],
                     wqkvT.rearrange("(dc p) o -> p dc o", p=128)[:, :, 2048:3072])
                LW_V = ("vq2", c.inc("vq2", 16))

        for item in pend:
            emit_rope(item)
        pend = []

        vidx = 0
        for tt in range(16):
            for sl in range(2):
                r = vidx % 4
                c.wait(Tn, slot_free_ac[r][0], slot_free_ac[r][1])
                c.wait(Tn, *LW_V)
                for dc in range(8):
                    c.op(Tn, "matmul", rot4[r],
                         XBa[:, dc * 2048 + tt * 128: dc * 2048 + (tt + 1) * 128],
                         wq_st[:, dc, 512 + sl * 512: 512 + (sl + 1) * 512],
                         start=(dc == 0), stop=(dc == 7))
                pe_v = c.inc("pe")
                c.wait(Ve, "pe", pe_v)
                c.wait(Ve, "ve", MEMSET_DONE)
                base = tt * 1040 + sl * 8 * 65
                c.op(Ve, "tensor_copy",
                     VBa[:, base:base + 520].rearrange("p (h x) -> p h x", x=65)[:, :, 0:64],
                     rot4[r].rearrange("p (h x) -> p h x", x=64))
                slot_free_ac[r] = ("ve", c.inc("ve"))
                vidx += 1
        V_DONE_PE = c.now("pe")
        ROPE_V_DONE = c.now("ve")
        ROPE_V_AC = c.now("ac")
        if debug:
            c.wait(Sy, "ve", ROPE_V_DONE)
            c.op(Sy, "dma_start", dbg["QB0"].ap(), QBa); c.inc("st", 16)
            c.op(Sy, "dma_start", dbg["KB0"].ap(), KBa); c.inc("st", 16)
            c.op(Sy, "dma_start", dbg["VB0"].ap(), VBa); c.inc("st", 16)

        c.wait(Gp, "pe", V_DONE_PE)
        c.op(Gp, "dma_start", wout_sb.rearrange("p (dc o) -> p dc o", o=1024),
             woutT.rearrange("(dc p) o -> p dc o", p=128))
        WOUT_LW = c.inc("wo2", 16)
        c.op(Gp, "dma_start", wiS.rearrange("p (dc f) -> p dc f", f=FF),
             wiTd.rearrange("(dc p) f -> p dc f", p=128))
        WIS_LW = c.inc("wis", 16)
        c.op(Gp, "dma_start", woS.rearrange("p (fc d) -> p fc d", d=D),
             woTd.rearrange("(fc p) d -> p fc d", p=128))
        WOS_LW = c.inc("wos", 16)

        av_prev_free = 0
        exp_free = [0, 0]
        state = {"psD_ve": 0, "psE_ve": 0, "sq_ve": 0}
        ve_y = [0, 0, 0, 0]
        ar_mark = [0, 0, 0, 0]
        z1_mark = [0, 0, 0, 0]
        ct_mark = [0, 0, 0, 0]
        tp_mark = [0, 0, 0, 0]
        av_prev_pe = 0

        def tail_gen(g, O_D):
            for grp in range(4):
                c.wait(Tn, "ve", state["psE_ve"])
                if grp == 0:
                    c.wait(Tn, "ve", O_D)
                for dc in range(8):
                    c.op(Tn, "transpose", psE[:, dc * 128:(dc + 1) * 128],
                         o_sb[:, grp * 1024 + dc * 128: grp * 1024 + (dc + 1) * 128],
                         idb)
                    if dc == 3:
                        yield
                pe_t = c.inc("pe")
                c.wait(Ve, "pe", pe_t)
                for dc in range(8):
                    c.op(Ve, "tensor_copy",
                         oT_sb[:, dc * 512 + grp * 128: dc * 512 + (grp + 1) * 128],
                         psE[:, dc * 128:(dc + 1) * 128])
                state["psE_ve"] = c.inc("ve")
                yield
            tp_mark[g] = state["psE_ve"]
            OT_DONE = state["psE_ve"]
            sq_guard = state["sq_ve"]
            for doc in range(8):
                c.wait(Tn, "wo2", WOUT_LW)
                c.wait(Tn, "ve", state["psD_ve"])
                if doc == 0:
                    c.wait(Tn, "ve", OT_DONE)
                for dmc in range(8):
                    c.op(Tn, "matmul", psD,
                         wout_sb[:, dmc * 1024 + doc * 128: dmc * 1024 + (doc + 1) * 128],
                         oT_sb[:, dmc * 512:(dmc + 1) * 512],
                         start=(dmc == 0), stop=(dmc == 7))
                    if dmc == 3:
                        yield
                pe_op = c.inc("pe")
                c.wait(Ve, "pe", pe_op)
                c.wait(Ve, "ve", sq_guard)
                if debug and g == 1 and doc == 0:
                    c.wait(Ve, "st", c.now("st"))
                c.op(Ve, "tensor_add",
                     z1[:, doc * 512:(doc + 1) * 512], psD,
                     XBa[:, doc * 2048 + g * 512: doc * 2048 + (g + 1) * 512])
                state["psD_ve"] = c.inc("ve")
                yield
            VE_Z = c.now("ve")
            z1_mark[g] = VE_Z
            ar_mark[g] = c.now("pe")
            if debug and g == 0:
                c.wait(Sy, "ve", VE_Z)
                c.op(Sy, "dma_start", dbg["Z1"].ap(), z1); c.inc("st", 16)
            c.wait(Ve, "ve", VE_Z)
            c.op(Ve, "tensor_mul", sqa[:], z1, z1)
            ac_q = c.inc("ve")
            state["sq_ve"] = ac_q
            c.wait(Tn, "ve", VE_Z)
            c.wait(Tn, "ve", state["psD_ve"])
            for dc in range(8):
                c.op(Tn, "matmul", psD[0:1, :], onescol,
                     z1[:, dc * 512:(dc + 1) * 512], start=(dc == 0), stop=(dc == 7))
            pe_s1 = c.inc("pe")
            c.wait(Ve, "pe", pe_s1)
            c.op(Ve, "tensor_scalar", r_mean, psD[0:1, :], 1.0 / D, None, ALU.mult)
            state["psD_ve"] = c.inc("ve")
            yield
            yield
            c.wait(Tn, "ve", ac_q)
            c.wait(Tn, "ve", state["psD_ve"])
            for dc in range(8):
                c.op(Tn, "matmul", psD[0:1, :], onescol,
                     sqa[:, dc * 512:(dc + 1) * 512], start=(dc == 0), stop=(dc == 7))
            pe_s2 = c.inc("pe")
            c.wait(Ve, "pe", pe_s2)
            c.op(Ve, "tensor_scalar", r_var, psD[0:1, :], 1.0 / D, None, ALU.mult)
            v1 = c.inc("ve")
            state["psD_ve"] = v1
            c.wait(Ve, "ve", v1)
            c.op(Ve, "tensor_mul", r_tmp, r_mean, r_mean)
            v2 = c.inc("ve")
            c.wait(Ve, "ve", v2)
            c.op(Ve, "tensor_sub", r_var, r_var, r_tmp)
            ve_var = c.inc("ve")
            c.wait(Sc, "ve", ve_var)
            c.op(Sc, "activation", r_tmp, r_var, AF.Ln, bias=eps)
            a1 = c.inc("ac")
            c.wait(Sc, "ac", a1)
            c.op(Sc, "activation", r_tmp, r_tmp, AF.Exp, bias=0.0, scale=-0.5)
            ac_rs = c.inc("ac")
            c.wait(Ve, "ac", ac_rs)
            c.op(Ve, "scalar_tensor_tensor", r_nmr, r_mean, -1.0, r_tmp,
                 ALU.mult, ALU.mult)
            ve_nm = c.inc("ve")
            yield
            yield
            yield
            c.wait(Tn, "ve", ve_nm)
            c.wait(Tn, "ve", state["psD_ve"])
            c.op(Tn, "matmul", psD, onesrow_f, r_tmp, start=True, stop=True)
            pe_b1 = c.inc("pe")
            c.wait(Ve, "pe", pe_b1)
            for dc in range(8):
                c.op(Ve, "tensor_mul", sqa[:, dc * 512:(dc + 1) * 512],
                     z1[:, dc * 512:(dc + 1) * 512], psD)
            state["psD_ve"] = c.inc("ve")
            ve_t1m = state["psD_ve"]
            yield
            c.wait(Tn, "ve", state["psD_ve"])
            c.op(Tn, "matmul", psD, onesrow_f, r_nmr, start=True, stop=True)
            pe_b2 = c.inc("pe")
            c.wait(Ve, "pe", pe_b2)
            c.wait(Ve, "ve", ve_t1m)
            for dc in range(8):
                c.op(Ve, "tensor_add",
                     QBa[:, dc * 2048 + g * 512: dc * 2048 + (g + 1) * 512],
                     sqa[:, dc * 512:(dc + 1) * 512], psD)
            ve_y[g] = c.inc("ve")
            state["psD_ve"] = ve_y[g]
            yield
            yield
            yield
            pe_ct_prev = 0
            for tt in range(4):
                c.wait(Tn, "ve", ve_y[g])
                c.wait(Tn, "ve", state["psD_ve"])
                for dc in range(8):
                    c.op(Tn, "matmul", psD[:, 0:40],
                         QBa[:, dc * 2048 + g * 512 + tt * 128:
                             dc * 2048 + g * 512 + (tt + 1) * 128],
                         gA[:, dc * 40:(dc + 1) * 40],
                         start=(dc == 0), stop=False, skip_group_check=True)
                c.op(Tn, "matmul", psD[:, 0:40], onesrow_b, brow,
                     start=False, stop=True, skip_group_check=True)
                pe_g = c.inc("pe")
                c.wait(Ve, "pe", pe_g)
                c.op(Ve, "tensor_copy", aux[:, 0:8], psD[:, 0:8])
                s1 = c.inc("ve"); c.wait(Ve, "ve", s1)
                c.op(Ve, "reduce_max", aux[:, 8:9], aux[:, 0:8], axis=X)
                s2 = c.inc("ve"); c.wait(Ve, "ve", s2)
                c.op(Ve, "tensor_scalar", mask_f, aux[:, 0:8], aux[:, 8:9], 0.0,
                     ALU.is_ge, ALU.add, accum_out=aux[:, 9:10])
                s3 = c.inc("ve"); c.wait(Ve, "ve", s3)
                c.op(Ve, "reciprocal", aux[:, 10:11], aux[:, 9:10])
                s4 = c.inc("ve"); c.wait(Ve, "ve", s4)
                c.op(Ve, "tensor_scalar", mask_b, mask_f, aux[:, 10:11], None, ALU.mult)
                s5 = c.inc("ve"); c.wait(Ve, "ve", s5)
                c.wait(Ve, "pe", pe_ct_prev)
                c.op(Ve, "tensor_tensor",
                     c_st.rearrange("p (e r) -> p e r", e=8),
                     psD[:, 8:40].rearrange("p (e r) -> p e r", e=8),
                     mask_b[:, :, None].broadcast_to([128, 8, 4]),
                     ALU.mult)
                ve_c = c.inc("ve")
                state["psD_ve"] = ve_c
                yield
                yield
                c.wait(Tn, "ve", ve_c)
                c.wait(Tn, "ve", state["psE_ve"])
                c.op(Tn, "transpose", psE[0:32, 0:128], c_st, idb)
                pe_ct = c.inc("pe")
                pe_ct_prev = pe_ct
                c.wait(Ve, "pe", pe_ct)
                c.op(Ve, "tensor_copy",
                     cT[:, g * 512 + tt * 128: g * 512 + (tt + 1) * 128],
                     psE[0:32, 0:128])
                state["psE_ve"] = c.inc("ve")
                ct_mark[g] = state["psE_ve"]
                yield

        tail = None
        pump = [0]
        for g in range(4):
            ktmax = 4 * g + 4
            if g == 0:
                c.wait(Tn, "ve", ROPE_V_DONE)
                c.wait(Tn, "ac", ROPE_V_AC)
            for cc in range(8):
                pe_s = {}
                ac_e = {}
                ve_m = {}
                av_done = {}

                def emit_av(kt):
                    pp = kt % 3
                    c.wait(Tn, "ac", ac_e[kt])
                    j = kt - 4 * g
                    if kt == 0:
                        c.wait(Tn, "ve", av_prev_free)
                    first_unmasked = 1 if (kt == 0 and j == 0) else 0
                    for masked in (False, True):
                        if masked:
                            if j < 0:
                                continue
                            c.wait(Tn, "ve", ve_m[kt])
                        for hl in (0, 1):
                            for qt in range(4):
                                if j >= 0 and qt < j:
                                    continue
                                if (qt == j) != masked:
                                    continue
                                col = hl * 512 + qt * 65
                                c.op(Tn, "matmul", psC[:, col: col + 65],
                                     p_sb[pp][:, hl * 512 + qt * 128:
                                               hl * 512 + (qt + 1) * 128],
                                     VBa[:, kt * 1040 + (2 * cc + hl) * 65:
                                         kt * 1040 + (2 * cc + hl) * 65 + 65],
                                     start=(kt == 0 and not masked
                                            and qt == first_unmasked),
                                     stop=(kt == ktmax - 1),
                                     skip_group_check=True)
                    av_done[kt] = c.inc("pe")

                for kt in range(ktmax):
                    pp2 = kt % 2
                    pp = kt % 3
                    psS = psA if pp2 == 0 else psB
                    c.wait(Tn, "ac", exp_free[pp2])
                    pump[0] += 1
                    for _ in range(2 if tail is None else 1):
                        c.op(Tn, "matmul", psS[:, 0:512], idb,
                             QBa[:, 0:512], start=True, stop=True)
                    for hl in (0, 1):
                        c.op(Tn, "matmul", psS[:, hl * 512:(hl + 1) * 512],
                             KBa[hl * 64:(hl + 1) * 64,
                                 cc * 2048 + kt * 128: cc * 2048 + (kt + 1) * 128],
                             QBa[hl * 64:(hl + 1) * 64,
                                 cc * 2048 + g * 512: cc * 2048 + (g + 1) * 512],
                             start=True, stop=True)
                    pe_s[kt] = c.inc("pe")
                    c.wait(Sc, "pe", pe_s[kt])
                    c.wait(Sc, "pe", av_done[kt - 3] if kt >= 3 else av_prev_pe)
                    jj = kt - 4 * g
                    if jj > 0:
                        c.op(Sc, "activation",
                             p_sb[pp].rearrange("p (h q) -> p h q", q=512)
                             [:, :, jj * 128:512],
                             psS.rearrange("p (h q) -> p h q", q=512)
                             [:, :, jj * 128:512],
                             AF.Exp, bias=0.0, scale=0.125)
                    else:
                        c.op(Sc, "activation", p_sb[pp], psS, AF.Exp,
                             bias=0.0, scale=0.125)
                    ac_e[kt] = c.inc("ac")
                    exp_free[pp2] = ac_e[kt]
                    if kt >= 4 * g:
                        j = kt - 4 * g
                        c.wait(Ve, "ac", ac_e[kt])
                        c.op(Ve, "tensor_tensor",
                             p_sb[pp].rearrange("p (h q) -> p h q", q=512)
                             [:, :, j * 128:(j + 1) * 128],
                             p_sb[pp].rearrange("p (h q) -> p h q", q=512)
                             [:, :, j * 128:(j + 1) * 128],
                             cmask[:, 384:512][:, None, :].broadcast_to([128, 2, 128]),
                             ALU.mult)
                        ve_m[kt] = c.inc("ve")
                    if tail is not None:
                        next(tail, None)
                    if kt >= 2:
                        emit_av(kt - 2)
                emit_av(ktmax - 2)
                emit_av(ktmax - 1)
                av_prev_pe = av_done[ktmax - 1]
                if g >= 1 and cc == 0:
                    if tail is not None:
                        while tp_mark[g - 1] == 0:
                            if next(tail, "END") == "END":
                                break
                c.wait(Ve, "pe", av_done[ktmax - 1])
                if cc == 0 and g >= 1:
                    c.wait(Ve, "ve", tp_mark[g - 1])
                    if debug:
                        c.wait(Ve, "st", c.now("st"))
                psC_v = (psC.rearrange("p (hl x) -> p hl x", hl=2)[:, :, 0:260]
                         .rearrange("p hl (qt d) -> p hl qt d", d=65))
                c.op(Ve, "reciprocal",
                     den.rearrange("p (hl qt) -> p hl qt", hl=2)[:, :, :, None],
                     psC_v[:, :, :, 64:65])
                ve_d = c.inc("ve")
                c.wait(Ve, "ve", ve_d)
                o_dst = (o_sb.rearrange("p (qt h2 d) -> p qt h2 d", h2=16, d=64)
                         [:, :, 2 * cc:2 * cc + 2, :].transpose([0, 2, 1, 3]))
                c.op(Ve, "tensor_tensor", o_dst,
                     psC_v[:, :, :, 0:64],
                     den.rearrange("p (hl qt) -> p hl qt", hl=2)[:, :, :, None]
                     .broadcast_to([128, 2, 4, 64]),
                     ALU.mult)
                av_prev_free = c.inc("ve")
            if debug and g == 0:
                c.wait(Sy, "ve", av_prev_free)
                c.op(Sy, "dma_start", dbg["OSB"].ap(), o_sb); c.inc("st", 16)
            if tail is not None:
                for _ in tail:
                    pass
            tail = tail_gen(g, c.now("ve"))

        c.wait(Gp, "pe", c.now("pe"))
        c.op(Gp, "dma_start", bcat, bcatTd.ap())
        BCAT_LW = c.inc("bct", 16)

        wi_buf_free = [c.now("pe"), c.now("pe")]
        wo_buf_free = [c.now("pe"), c.now("pe")]
        lw_wi = {}
        lw_wo = {}
        nxt_lw0 = []
        relu_free = [exp_free[0], exp_free[0], exp_free[1], exp_free[1]]
        z2ev = [av_prev_free, av_prev_free]
        ve_y2 = 0
        ST_PREV = 0
        wiS_v = wiS.rearrange("p (dc f) -> p dc f", f=FF)
        woS_v = woS.rearrange("p (fc d) -> p fc d", d=D)
        st_box = [0]

        def ln2_gen(g, VE_Z2):
            c.wait(Ve, "ve", VE_Z2)
            c.op(Ve, "tensor_mul", sqa[:], z2, z2)
            ve_q2 = c.inc("ve")
            c.wait(Tn, "ve", VE_Z2)
            c.wait(Tn, "ve", state["psD_ve"])
            for dc in range(8):
                c.op(Tn, "matmul", psD[0:1, :], onescol,
                     z2[:, dc * 512:(dc + 1) * 512], start=(dc == 0), stop=(dc == 7))
            pe_s1 = c.inc("pe")
            c.wait(Ve, "pe", pe_s1)
            c.op(Ve, "tensor_scalar", r_mean, psD[0:1, :], 1.0 / D, None, ALU.mult)
            state["psD_ve"] = c.inc("ve")
            yield
            c.wait(Tn, "ve", ve_q2)
            c.wait(Tn, "ve", state["psD_ve"])
            for dc in range(8):
                c.op(Tn, "matmul", psD[0:1, :], onescol,
                     sqa[:, dc * 512:(dc + 1) * 512], start=(dc == 0), stop=(dc == 7))
            pe_s2 = c.inc("pe")
            c.wait(Ve, "pe", pe_s2)
            c.op(Ve, "tensor_scalar", r_var, psD[0:1, :], 1.0 / D, None, ALU.mult)
            v1 = c.inc("ve")
            state["psD_ve"] = v1
            c.wait(Ve, "ve", v1)
            c.op(Ve, "tensor_mul", r_tmp, r_mean, r_mean)
            v2 = c.inc("ve")
            c.wait(Ve, "ve", v2)
            c.op(Ve, "tensor_sub", r_var, r_var, r_tmp)
            ve_var = c.inc("ve")
            c.wait(Sc, "ve", ve_var)
            c.op(Sc, "activation", r_tmp, r_var, AF.Ln, bias=eps)
            a1 = c.inc("ac")
            c.wait(Sc, "ac", a1)
            c.op(Sc, "activation", r_tmp, r_tmp, AF.Exp, bias=0.0, scale=-0.5)
            ac_rs = c.inc("ac")
            c.wait(Ve, "ac", ac_rs)
            c.op(Ve, "scalar_tensor_tensor", r_nmr, r_mean, -1.0, r_tmp,
                 ALU.mult, ALU.mult)
            ve_nm = c.inc("ve")
            yield
            yield
            c.wait(Tn, "ve", ve_nm)
            c.wait(Tn, "ve", state["psD_ve"])
            c.op(Tn, "matmul", psD, onesrow_f, r_tmp, start=True, stop=True)
            pe_b1 = c.inc("pe")
            c.wait(Ve, "pe", pe_b1)
            c.wait(Ve, "st", st_box[0])
            for dc in range(8):
                c.op(Ve, "tensor_mul", sqa[:, dc * 512:(dc + 1) * 512],
                     z2[:, dc * 512:(dc + 1) * 512], psD)
            ve_t2 = c.inc("ve")
            state["psD_ve"] = ve_t2
            yield
            c.wait(Tn, "ve", state["psD_ve"])
            c.op(Tn, "matmul", psD, onesrow_f, r_nmr, start=True, stop=True)
            pe_b2 = c.inc("pe")
            c.wait(Ve, "pe", pe_b2)
            c.wait(Ve, "ve", ve_t2)
            for dc in range(8):
                c.op(Ve, "tensor_add", y2[:, dc * 512:(dc + 1) * 512],
                     sqa[:, dc * 512:(dc + 1) * 512], psD)
            ve_y2 = c.inc("ve")
            state["psD_ve"] = ve_y2
            c.wait(Sy, "ve", ve_y2)
            c.op(Sy, "dma_start",
                 outT.rearrange("(dc p) t -> p dc t", p=128)[:, :, g * 512:(g + 1) * 512],
                 y2.rearrange("p (dc t) -> p dc t", t=512))
            st_box[0] = c.inc("st", 16)
            yield

        ln2 = None
        for g in range(4):
            if g == 0:
                for e8 in (0, 1):
                    c.wait(Sy, "pe", wi_buf_free[e8 % 2])
                    c.wait(Sy, "wis", WIS_LW)
                    c.op(Sy, "dma_start",
                         wi_buf[e8 % 2].rearrange("p (dc f) -> p dc f", f=512),
                         wiS_v[:, :, e8 * 512:(e8 + 1) * 512])
                    lw_wi[e8] = ("li%d" % (e8 % 2), c.inc("li%d" % (e8 % 2), 16))
            if debug and g == 1:
                c.wait(Sc, "st", c.now("st"))
            if tail is not None:
                while z1_mark[3] == 0:
                    if next(tail, "END") == "END":
                        break
            for e8 in range(8):
                if e8 == 0 and g >= 1:
                    for _ in range(6):
                        c.op(Tn, "matmul", rot4[0], idb, XBa[:, 0:512],
                             start=True, stop=True)
                c.wait(Tn, *lw_wi[e8])
                c.wait(Tn, "bct", BCAT_LW)
                for fcl in range(4):
                    fc = e8 * 4 + fcl
                    r = fc % 4
                    c.wait(Tn, "ac", relu_free[r])
                    c.wait(Tn, "ve", ve_y[g])
                    c.wait(Tn, "ve", ct_mark[g])
                    for dc in range(8):
                        c.op(Tn, "matmul", rot4[r],
                             wi_buf[e8 % 2][:, dc * 512 + fcl * 128:
                                            dc * 512 + (fcl + 1) * 128],
                             QBa[:, dc * 2048 + g * 512: dc * 2048 + (g + 1) * 512],
                             start=(dc == 0), stop=False)
                    c.op(Tn, "matmul", rot4[r],
                         bcat[:, fc * 128:(fc + 1) * 128],
                         cT[:, g * 512:(g + 1) * 512],
                         start=False, stop=True)
                    pe_wi = c.inc("pe")
                    c.wait(Sc, "pe", pe_wi)
                    if g == 0 and fc == 0:
                        c.wait(Sc, "ve", z1_mark[3])
                    c.op(Sc, "activation", XBa[:, fc * 512:(fc + 1) * 512],
                         rot4[r], AF.Relu, bias=wib2[:, fc:fc + 1])
                    relu_free[r] = c.inc("ac")
                    if tail is not None:
                        next(tail, None)
                    if ln2 is not None:
                        next(ln2, None)
                wi_buf_free[e8 % 2] = c.now("pe")
                nxt = e8 + 2
                if nxt <= 7:
                    c.wait(Sy, "pe", wi_buf_free[nxt % 2])
                    c.op(Sy, "dma_start",
                         wi_buf[nxt % 2].rearrange("p (dc f) -> p dc f", f=512),
                         wiS_v[:, :, nxt * 512:(nxt + 1) * 512])
                    lw_wi[nxt] = ("li%d" % (nxt % 2), c.inc("li%d" % (nxt % 2), 16))
                elif g < 3:
                    ne = nxt - 8
                    c.wait(Sy, "pe", wi_buf_free[ne % 2])
                    c.op(Sy, "dma_start",
                         wi_buf[ne % 2].rearrange("p (dc f) -> p dc f", f=512),
                         wiS_v[:, :, ne * 512:(ne + 1) * 512])
                    lw_wi[ne] = ("li%d" % (ne % 2), c.inc("li%d" % (ne % 2), 16))
            if tail is not None:
                for _ in tail:
                    pass
                tail = None
                wo_buf_free[0] = c.now("pe")
                wo_buf_free[1] = c.now("pe")
            if ln2 is not None:
                for _ in ln2:
                    pass
                ln2 = None
                if debug:
                    c.wait(Sy, "ve", c.now("ve"))
                    c.op(Sy, "dma_start", dbg["Y1"].ap(), QBa); c.inc("st", 16)
                    c.op(Sy, "dma_start", dbg["CT"].ap(), cT); c.inc("st", 16)

            if debug and g == 0:
                c.wait(Sy, "ac", c.now("ac"))
                c.op(Sy, "dma_start", dbg["INT"].ap(), XBa); c.inc("st", 16)
            if g == 0:
                c.wait(Sy, "pe", wo_buf_free[0])
                c.wait(Sy, "pe", ar_mark[3])
                c.wait(Sy, "wos", WOS_LW)
                c.op(Sy, "dma_start",
                     wo_buf[0].rearrange("p (fc d) -> p fc d", d=256),
                     woS_v[:, :, 0:256])
                lw_wo[0] = ("lo0", c.inc("lo0", 16))
            for doc in range(8):
                dp = doc // 2
                if doc % 2 == 0 and dp + 1 <= 3:
                    c.wait(Sy, "pe", wo_buf_free[(dp + 1) % 2])
                    c.op(Sy, "dma_start",
                         wo_buf[(dp + 1) % 2].rearrange("p (fc d) -> p fc d", d=256),
                         woS_v[:, :, (dp + 1) * 256:(dp + 2) * 256])
                    lw_wo[dp + 1] = ("lo%d" % ((dp + 1) % 2), c.inc("lo%d" % ((dp + 1) % 2), 16))
                if doc == 6 and g < 3:
                    nxt_lw0.append(c.now("pe"))
                    c.wait(Sy, "pe", nxt_lw0[-1])
                    c.op(Sy, "dma_start",
                         wo_buf[0].rearrange("p (fc d) -> p fc d", d=256),
                         woS_v[:, :, 0:256])
                    lw_wo[0] = ("lo0", c.inc("lo0", 16))
                c.wait(Tn, *lw_wo[dp])
                h2 = doc % 2
                c.wait(Tn, "ve", z2ev[h2])
                if doc == 0:
                    for _ in range(6):
                        c.op(Tn, "matmul", psC_h[h2], idb, XBa[:, 0:512],
                             start=True, stop=True)
                c.wait(Tn, "ac", c.now("ac") if doc == 0 else 0)
                for fc in range(32):
                    c.op(Tn, "matmul", psC_h[h2],
                         wo_buf[dp % 2][:, fc * 256 + (doc % 2) * 128:
                                        fc * 256 + (doc % 2) * 128 + 128],
                         XBa[:, fc * 512:(fc + 1) * 512],
                         start=(fc == 0), stop=(fc == 31))
                pe_wo = c.inc("pe")
                if doc % 2 == 1:
                    wo_buf_free[dp % 2] = pe_wo
                c.wait(Ve, "pe", pe_wo)
                c.op(Ve, "scalar_tensor_tensor",
                     z2[:, doc * 512:(doc + 1) * 512],
                     psC_h[h2], wob2[:, doc:doc + 1],
                     QBa[:, doc * 2048 + g * 512: doc * 2048 + (g + 1) * 512],
                     ALU.add, ALU.add)
                z2ev[h2] = c.inc("ve")
            VE_Z2 = c.now("ve")
            if debug and g == 0:
                c.wait(Sy, "ve", VE_Z2)
                c.op(Sy, "dma_start", dbg["Z2"].ap(), z2); c.inc("st", 16)
            if debug and g == 1:
                c.wait(Ve, "st", c.now("st"))
            ln2 = ln2_gen(g, VE_Z2)
            if g == 3:
                for _ in ln2:
                    pass
                ln2 = None

        c.wait(Sy, "st", c.now("st"))

    ctx = Ctx(nc, sems)
    with nc.Block() as block:
        @block.tensor
        def _(eng):
            ctx.run("tensor", eng, emit)

        @block.scalar
        def _(eng):
            ctx.run("scalar", eng, emit)

        @block.vector
        def _(eng):
            ctx.run("vector", eng, emit)

        @block.gpsimd
        def _(eng):
            ctx.run("gpsimd", eng, emit)

        @block.sync
        def _(eng):
            ctx.run("sync", eng, emit)

    return nc


def _constants():
    bf = ml_dtypes.bfloat16
    ident = np.eye(128, dtype=bf)
    rp = np.zeros((128, 128), dtype=np.float32)
    for i in range(64):
        rp[2 * i + 1, 2 * i] = 1.0
        rp[2 * i, 2 * i + 1] = 1.0
    freqs = (1.0 / 10000.0) ** (np.arange(0, DH, 2, dtype=np.float64) / DH)
    t = np.arange(T_LEN, dtype=np.float64)
    ang = np.outer(t, freqs)
    cosv = np.cos(ang)
    sinv = np.sin(ang)
    rcos = np.zeros((128, T_LEN), dtype=np.float32)
    rsin = np.zeros((128, T_LEN), dtype=np.float32)
    for f in range(128):
        i = (f % 64) // 2
        rcos[f] = cosv[:, i]
        rsin[f] = sinv[:, i] * (-1.0 if f % 2 == 0 else 1.0)
    cm = np.zeros((128, 896), dtype=np.float32)
    for p in range(128):
        cm[p, 384 + p:] = 1.0
    return (ident, rp.astype(bf), rcos.astype(bf), rsin.astype(bf), cm.astype(bf))


_NC_CACHE = None


def kernel(L0_attn_x, L0_attn_W_qkv, L0_attn_W_out, L0_ffn_gate_W, L0_ffn_gate_b,
           L0_ffn_wi_W, L0_ffn_wi_b, L0_ffn_wo_W, L0_ffn_wo_b,
           L0_ffn_lora_A, L0_ffn_lora_B):
    global _NC_CACHE
    x = np.asarray(L0_attn_x, dtype=np.float32)
    W_qkv = np.asarray(L0_attn_W_qkv, dtype=np.float32)
    W_out = np.asarray(L0_attn_W_out, dtype=np.float32)
    gate_W = np.asarray(L0_ffn_gate_W, dtype=np.float32)
    gate_b = np.asarray(L0_ffn_gate_b, dtype=np.float32)
    wi_W = np.asarray(L0_ffn_wi_W, dtype=np.float32)
    wi_b = np.asarray(L0_ffn_wi_b, dtype=np.float32)
    wo_W = np.asarray(L0_ffn_wo_W, dtype=np.float32)
    wo_b = np.asarray(L0_ffn_wo_b, dtype=np.float32)
    lora_A = np.asarray(L0_ffn_lora_A, dtype=np.float32)
    lora_B = np.asarray(L0_ffn_lora_B, dtype=np.float32)

    wqkvT = np.ascontiguousarray(W_qkv.T)
    woutT = np.ascontiguousarray(W_out.T)
    gA = np.ascontiguousarray(
        np.concatenate([gate_W.T, lora_A.reshape(32, 1024).T], axis=1))
    brow = np.concatenate([gate_b, np.zeros(32, np.float32)]).reshape(1, 40)
    bcatT = np.ascontiguousarray(lora_B.transpose(0, 2, 1).reshape(32, FF))
    wiT = np.ascontiguousarray(wi_W.T)
    wib2 = np.ascontiguousarray(wi_b.reshape(32, 128).T)
    woT = np.ascontiguousarray(wo_W.T)
    wob2 = np.ascontiguousarray(wo_b.reshape(8, 128).T)
    ident, rp, rcos, rsin, cmask = _constants()

    if _NC_CACHE is None:
        _NC_CACHE = build_program()
    nc = _NC_CACHE

    shared = dict(wqkvT=wqkvT, woutT=woutT, gA=gA, brow=brow, bcatT=bcatT,
                  wiT=wiT, wib2=wib2, woT=woT, wob2=wob2,
                  idb=ident, rp=rp, rcos=rcos, rsin=rsin, cmask=cmask)
    in_maps = [dict(shared, xT=np.ascontiguousarray(x[b].T)) for b in range(NCORES)]
    res = run_bass_kernel_spmd(nc, in_maps, core_ids=list(range(NCORES)))
    out = np.stack([np.ascontiguousarray(r["outT"].T) for r in res.results], axis=0)
    return out
